# version 12
# baseline (speedup 1.0000x reference)
import math
from contextlib import ExitStack

import numpy as np
import ml_dtypes

import concourse.bass as bass
import concourse.mybir as mybir
from concourse.bass_utils import run_bass_kernel_spmd

F32 = mybir.dt.float32
BF16 = mybir.dt.bfloat16
FP8 = mybir.dt.float8e4
AF = mybir.ActivationFunctionType
ALU = mybir.AluOpType
AX = mybir.AxisListType

B, S, E, H, G, D = 4, 4096, 2048, 16, 4, 128
II = 5632
NTS = S // 128
NSLOT = 16
EC = E // 128
ICH = II // 128
NSL = 11
SCL = 1.0 / math.sqrt(D)
CSHIFT = 20.0
EPS = 1e-5
VST = 136
NEG = -1.0e30


def kstar(s):
    return 2 * s + 2


DEBUG_DUMP = False
DEBUG_SERIAL = False
DEBUG_SERIAL_B = False
DEBUG_SERIAL_C = False
STOP_AFTER = ""


class Prog:
    def __init__(self):
        nc = bass.Bass("TRN2", target_bir_lowering=False, detect_race_conditions=True)
        self.nc = nc
        d = nc.dram_tensor
        self.xseq = d("xseq", [S, E], F32, kind="ExternalInput").ap()
        self.xown = d("xown", [NSLOT * 128, E], F32, kind="ExternalInput").ap()
        self.wq = d("wq", [H, E, D], F32, kind="ExternalInput").ap()
        self.bq = d("bq", [D, H], F32, kind="ExternalInput").ap()
        self.wk = d("wk", [G, E, D], F32, kind="ExternalInput").ap()
        self.bk = d("bk", [D, G], F32, kind="ExternalInput").ap()
        self.wv = d("wv", [G, E, D], F32, kind="ExternalInput").ap()
        self.bvb = d("bvb", [G, 128, D], F32, kind="ExternalInput").ap()
        self.wg = d("wg", [II, E], F32, kind="ExternalInput").ap()
        self.wu = d("wu", [II, E], F32, kind="ExternalInput").ap()
        self.wd = d("wd", [E, II], F32, kind="ExternalInput").ap()
        self.mk = d("mk", [128, NSLOT * 2 * 128], F32, kind="ExternalInput").ap()
        self.iden = d("iden", [128, 128], BF16, kind="ExternalInput").ap()
        sk = "ExternalOutput" if DEBUG_DUMP else "Internal"
        self.attn_s = d("attn_s", [NSLOT * 128, E], F32, kind=sk).ap()
        self.t_s = d("t_s", [NSLOT * 128, II], F32, kind=sk).ap()
        self.y_s = d("y_s", [NSLOT * 128, E], F32, kind=sk).ap()
        self.out = d("out", [NSLOT * 128, E], F32, kind="ExternalOutput").ap()
        if DEBUG_DUMP:
            self.dbg_q = d("dbg_q", [128, NSLOT * 2048], F32, kind="ExternalOutput").ap()
            self.dbg_k = d("dbg_k", [128, G * S], F32, kind="ExternalOutput").ap()
            self.dbg_v = d("dbg_v", [128, G * NTS * VST], F32, kind="ExternalOutput").ap()
            self.dbg_a = d("dbg_a", [128, 3 * 4096 + 64], F32, kind="ExternalOutput").ap()
            self.dbg_b = d("dbg_b", [128, 4096], F32, kind="ExternalOutput").ap()
            self.dbg_c = d("dbg_c", [128, 2048], F32, kind="ExternalOutput").ap()
            self.dbg_d = d("dbg_d", [128, EC * 2048 + 2 * EC * 512 + 64 + NSLOT], F32,
                           kind="ExternalOutput").ap()

        self.es = ExitStack()
        self.sems = {}
        self.cnt = {}
        for n in ("sv", "ss", "st", "dini", "dwkv", "dxa0", "dxa1", "dwq0", "dwq1",
                  "dmk", "datt0", "datt1", "dwg", "dwu", "dtsl", "dwd", "dtt",
                  "dy0", "dy1", "dfx0", "dfx1", "dfo0", "dfo1",
                  "dfa0", "dfa1", "dfy0", "dfy1"):
            self.sems[n] = self.es.enter_context(nc.semaphore(n))
            self.cnt[n] = 0

    def sb(self, es, name, shape, dtype):
        return es.enter_context(self.nc.sbuf_tensor(name, shape, dtype))

    def ps(self, es, name, shape, dtype=F32):
        return es.enter_context(self.nc.psum_tensor(name, shape, dtype))

    def inc(self, inst, name, amt=1):
        inst.then_inc(self.sems[name], amt)
        self.cnt[name] += amt
        return self.cnt[name]

    def wait(self, eng, name, val):
        if val > 0:
            eng.wait_ge(self.sems[name], val)

    def serial_barrier(self):
        blk = self.block
        snap = dict(self.cnt)
        for ename in ("sync", "vector", "scalar", "tensor", "gpsimd"):
            def f(eng, snap=snap):
                for n, v in snap.items():
                    self.wait(eng, n, v)
            getattr(blk, ename)(f)

    def fence(self, inst, eng, name):
        v = self.inc(inst, name)
        eng.wait_ge(self.sems[name], v)
        return v

    def flat_ap(self, handle, off, pairs):
        return bass.AP(handle, off, pairs)

    def tp4(self, src_sb, c0, n, writer, stride=128):
        blk = self.block
        key = self._tp_idx % 2
        self._tp_idx += 1
        pp = self.ptp[key]
        guard = self._tp_guard.get(key, 0)

        @blk.tensor
        def _(t):
            self.wait(t, "sv", guard)
            i = None
            for q in range(n):
                i = t.transpose(pp[:, q * 128:(q + 1) * 128],
                                src_sb[:, c0 + q * stride: c0 + q * stride + 128], self.ident[:])
            self.inc(i, "st")
        stv = self.cnt["st"]

        @blk.vector
        def _(v):
            self.wait(v, "st", stv)
            i = writer(v, pp)
            self._tp_guard[key] = self.inc(i, "sv")

    def tp4list(self, src_sb, cols, writer):
        blk = self.block
        key = self._tp_idx % 2
        self._tp_idx += 1
        pp = self.ptp[key]
        guard = self._tp_guard.get(key, 0)

        @blk.tensor
        def _(t):
            self.wait(t, "sv", guard)
            i = None
            for q, c in enumerate(cols):
                i = t.transpose(pp[:, q * 128:(q + 1) * 128],
                                src_sb[:, c:c + 128], self.ident[:])
            self.inc(i, "st")
        stv = self.cnt["st"]

        @blk.vector
        def _(v):
            self.wait(v, "st", stv)
            i = writer(v, pp)
            self._tp_guard[key] = self.inc(i, "sv")

    def ln_stats(self, xg, ntile, prev_guard, o=0, jb=0):
        blk, stat = self.block, self.stat

        @blk.vector
        def _(v):
            self.wait(v, "ss", prev_guard)
            i = None
            for j in range(ntile):
                i = v.reduce_sum(out=stat[:, 24 + o + j:25 + o + j],
                                 in_=xg[:, j * 2048:(j + 1) * 2048], axis=AX.X)
            self.fence(i, v, "sv")
            i = v.tensor_scalar_mul(out=stat[:, 16 + o:16 + o + ntile],
                                    in0=stat[:, 24 + o:24 + o + ntile],
                                    scalar1=1.0 / 2048.0)
            self.inc(i, "sv")
        v_sum = self.cnt["sv"]

        for j in range(ntile):
            @blk.scalar
            def _(s, j=j):
                if j == 0:
                    self.wait(s, "sv", max(v_sum, self._jsq_free[jb]))
                i = s.activation(out=self.junk_s[:, (jb + j) * 1024:(jb + j + 1) * 1024],
                                 in_=xg[:, j * 2048: j * 2048 + 1024], func=AF.Square)
                i = s.activation(out=self.junk_s2[:, (jb + j) * 1024:(jb + j + 1) * 1024],
                                 in_=xg[:, j * 2048 + 1024: (j + 1) * 2048], func=AF.Square)
                self.inc(i, "ss")
        s_sq = self.cnt["ss"]

        @blk.vector
        def _(v):
            self.wait(v, "ss", s_sq)
            i = None
            for j in range(ntile):
                v.reduce_sum(out=stat[:, 32 + o + j:33 + o + j],
                             in_=self.junk_s[:, (jb + j) * 1024:(jb + j + 1) * 1024], axis=AX.X)
                i = v.reduce_sum(out=stat[:, 40 + o + j:41 + o + j],
                                 in_=self.junk_s2[:, (jb + j) * 1024:(jb + j + 1) * 1024], axis=AX.X)
            self.fence(i, v, "sv")
            i = v.tensor_add(out=stat[:, 32 + o:32 + o + ntile], in0=stat[:, 32 + o:32 + o + ntile],
                             in1=stat[:, 40 + o:40 + o + ntile])
            self._jsq_free[jb] = self.fence(i, v, "sv")
            v.tensor_scalar_mul(out=stat[:, 40 + o:40 + o + ntile],
                                in0=stat[:, 32 + o:32 + o + ntile],
                                scalar1=1.0 / 2048.0)
            i = v.tensor_mul(out=stat[:, 56 + o:56 + o + ntile], in0=stat[:, 16 + o:16 + o + ntile],
                             in1=stat[:, 16 + o:16 + o + ntile])
            self.fence(i, v, "sv")
            i = v.tensor_sub(out=stat[:, 56 + o:56 + o + ntile],
                             in0=stat[:, 40 + o:40 + o + ntile],
                             in1=stat[:, 56 + o:56 + o + ntile])
            self.inc(i, "sv")
        v_var = self.cnt["sv"]

        @blk.scalar
        def _(s):
            self.wait(s, "sv", v_var)
            i = s.activation(out=stat[:, 8 + o:8 + o + ntile], in_=stat[:, 56 + o:56 + o + ntile],
                             func=AF.Sqrt, bias=self.consts[:, 0:1])
            self.inc(i, "ss")
        s_sqrt = self.cnt["ss"]

        @blk.vector
        def _(v):
            self.wait(v, "ss", s_sqrt)
            i = v.reciprocal(out=stat[:, o:o + ntile], in_=stat[:, 8 + o:8 + o + ntile])
            self.fence(i, v, "sv")
            i = v.tensor_mul(out=stat[:, 48 + o:48 + o + ntile], in0=stat[:, 16 + o:16 + o + ntile],
                             in1=stat[:, o:o + ntile])
            self.fence(i, v, "sv")
            i = v.tensor_scalar_mul(out=stat[:, 48 + o:48 + o + ntile],
                                    in0=stat[:, 48 + o:48 + o + ntile], scalar1=-1.0)
            self.inc(i, "sv")
        return self.cnt["sv"]

    def qscales(self, src_ap, prev_guard, wait_kind=None, wait_val=0, base=17):
        blk, stat = self.block, self.stat

        @blk.vector
        def _(v):
            if wait_kind:
                self.wait(v, wait_kind, wait_val)
            self.wait(v, "ss", prev_guard)
            i = v.reduce_max(out=stat[:, base:base + 1], in_=src_ap, axis=AX.X,
                             apply_absolute_value=True)
            self.fence(i, v, "sv")
            i = v.reciprocal(out=stat[:, base + 1:base + 2], in_=stat[:, base:base + 1])
            self.fence(i, v, "sv")
            v.tensor_scalar_mul(out=stat[:, base + 2:base + 3],
                                in0=stat[:, base + 1:base + 2], scalar1=224.0)
            i = v.tensor_scalar_mul(out=stat[:, base + 3:base + 4],
                                    in0=stat[:, base:base + 1], scalar1=2.0 / 448.0)
            self.inc(i, "sv")
        return self.cnt["sv"]

    def build(self):
        nc, es = self.nc, self.es
        self.ident = self.sb(es, "ident", [128, 128], BF16)
        self.bq_sb = self.sb(es, "bq_sb", [128, H], F32)
        self.bk_sb = self.sb(es, "bk_sb", [128, G], F32)
        self.bvb_sb = self.sb(es, "bvb_sb", [128, G * 128], F32)
        self.stat = self.sb(es, "stat", [128, 64], F32)
        self.junk_s = self.sb(es, "junk_s", [128, 2048], F32)
        self.junk_s2 = self.sb(es, "junk_s2", [128, 2048], F32)
        self._jsq_free = [0, 0]
        self.tmax = self.sb(es, "tmax", [128, NSLOT], F32)
        self.consts = self.sb(es, "consts", [128, 2], F32)

        self._tp_idx = 0
        self._tp_guard = {}

        self.block = es.enter_context(nc.Block())
        blk = self.block

        @blk.sync
        def _(sy):
            i = sy.dma_start(out=self.ident[:], in_=self.iden)
            self.inc(i, "dini", 16)
            i = sy.dma_start(out=self.bq_sb[:], in_=self.bq)
            self.inc(i, "dini", 16)
            i = sy.dma_start(out=self.bk_sb[:], in_=self.bk)
            self.inc(i, "dini", 16)
            i = sy.dma_start(out=self.bvb_sb[:].rearrange("p (g d) -> p g d", d=128),
                             in_=self.bvb.rearrange("g p d -> p g d"))
            self.inc(i, "dini", 16)
        self.setup_dx = self.cnt["dini"]

        @blk.vector
        def _(v):
            v.memset(self.consts[:, 0:1], EPS)
            i = v.memset(self.consts[:, 1:2], -CSHIFT)
            self.inc(i, "sv")
        self.consts_sv = self.cnt["sv"]

        @blk.scalar
        def _(s):
            s.preload_act_table(AF.Sqrt)

        r1 = ExitStack()
        self.K_sb = self.sb(r1, "K_sb", [128, G * S], BF16)
        self.V_sb = self.sb(r1, "V_sb", [128, G * NTS * VST], BF16)
        self.phase_A(r1)
        if not STOP_AFTER.startswith("A"):
            self.phase_B(r1)
        r1.close()
        if not (STOP_AFTER.startswith("A") or STOP_AFTER == "B"):
            r2 = ExitStack()
            self.fmT = self.sb(r2, "fmT", [128, EC * 2048], BF16)
            self.phase_C(r2)
            r2.close()
        if not (STOP_AFTER.startswith("A") or STOP_AFTER in ("B", "C")):
            r3 = ExitStack()
            self.phase_D(r3)
            r3.close()
        if STOP_AFTER == "":
            r4 = ExitStack()
            self.phase_F(r4)
            r4.close()
        if STOP_AFTER:
            blk = self.block

            @blk.gpsimd
            def _(gg):
                for n in ("st", "ss", "sv"):
                    self.wait(gg, n, self.cnt[n])
        es.close()
        return nc

    def phase_A(self, r1):
        blk = self.block
        a1 = ExitStack()
        wkv_f = self.sb(a1, "wkv_f", [128, 2048], F32)
        wk_sb = self.sb(a1, "wk_sb", [128, G * EC * 128], BF16)
        wv_sb = self.sb(a1, "wv_sb", [128, G * EC * 128], BF16)
        xg = [self.sb(a1, f"xg{i}", [128, 2 * 2048], F32) for i in range(2)]
        xnb = [self.sb(a1, f"xnb{i}", [128, 2 * 2048], BF16) for i in range(2)]
        xnT = [self.sb(a1, f"xnT{i}", [128, EC * 256], BF16) for i in range(2)]
        self.ptp = [self.ps(a1, f"ptpA{i}", [128, 512], BF16) for i in range(2)]
        psA = [self.ps(a1, f"psA{i}", [128, 512]) for i in range(2)]
        psB = [self.ps(a1, f"psB{i}", [128, 512]) for i in range(2)]
        self._tp_guard = {}

        wkv_done = 0
        for g in range(2 * G):
            w_ap = (self.wk if g < G else self.wv)[g % G]
            dst = (wk_sb if g < G else wv_sb)

            @blk.sync
            def _(sy, w_ap=w_ap, wkv_done=wkv_done):
                self.wait(sy, "sv", wkv_done)
                i = sy.dma_start(out=wkv_f[:], in_=w_ap.rearrange("(a p) d -> p a d", p=128))
                self.inc(i, "dwkv", 16)
            dd = self.cnt["dwkv"]

            @blk.vector
            def _(v, dst=dst, g=g, dd=dd):
                self.wait(v, "dwkv", dd)
                i = v.tensor_copy(out=dst[:, (g % G) * 2048:((g % G) + 1) * 2048], in_=wkv_f[:])
                self.inc(i, "sv")
            wkv_done = self.cnt["sv"]

        @blk.vector
        def _(v):
            v.memset(self.V_sb[:].rearrange("p (c x) -> p c x", x=VST)[:, :, 129:VST], 0.0)
            i = v.memset(self.V_sb[:].rearrange("p (c x) -> p c x", x=VST)[:, :, 128:129], 1.0)
            self.inc(i, "sv")
        vready = self.cnt["sv"]
        if STOP_AFTER == "A0":
            a1.close()
            return

        xg_cons = [0, 0]
        xnb_cons = [0, 0]
        k_evac = [0, 0]
        v_evac = [0, 0]
        ln_guard = 0
        ln_map = {}
        xn_map = {}
        xnT_map = {}

        def emit_aload(gi):
            bi = gi % 2

            @blk.sync
            def _(sy, gi=gi, bi=bi, g=xg_cons[bi]):
                self.wait(sy, "ss", g)
                i = sy.dma_start(
                    out=xg[bi][:].rearrange("p (j e) -> p j e", e=2048),
                    in_=self.xseq[gi * 256:(gi + 1) * 256, :].rearrange("(j p) e -> p j e", p=128))
                self.inc(i, "dxa%d" % bi, 16)

        def emit_aln(gi):
            bi = gi % 2
            dd = self.cnt["dxa%d" % bi]

            @blk.vector
            def _(v, bi=bi, dd=dd):
                self.wait(v, "dxa%d" % bi, dd)
            ln_map[gi] = self.ln_stats(xg[bi], 2, ln_guard)

        def emit_axn(gi):
            nonlocal ln_guard
            bi = gi % 2

            @blk.scalar
            def _(s, bi=bi, ln_done=ln_map[gi], g=xnb_cons[bi]):
                self.wait(s, "sv", ln_done)
                self.wait(s, "st", g)
                for j in range(2):
                    i = s.activation(out=xnb[bi][:, j * 2048:(j + 1) * 2048],
                                     in_=xg[bi][:, j * 2048:(j + 1) * 2048], func=AF.Identity,
                                     scale=self.stat[:, j:j + 1], bias=self.stat[:, 48 + j:49 + j])
                    self.inc(i, "ss")
            xn_map[gi] = self.cnt["ss"]
            xg_cons[bi] = self.cnt["ss"]
            ln_guard = self.cnt["ss"]

        def emit_atrans(gi):
            bi = gi % 2

            @blk.tensor
            def _(t, xn_done=xn_map[gi]):
                self.wait(t, "ss", xn_done)
                self.wait(t, "dini", self.setup_dx)
            for ec2 in range(0, EC, 2):
                def wr(v, pp, bi=bi, ec2=ec2):
                    return v.tensor_copy(out=xnT[bi][:, ec2 * 256: ec2 * 256 + 512], in_=pp[:])
                cols = [(ec2 + q // 2) * 128 + (q % 2) * 2048 for q in range(4)]
                self.tp4list(xnb[bi], cols, wr)
            xnb_cons[bi] = self.cnt["st"]
            xnT_map[gi] = self.cnt["sv"]

        emit_aload(0)
        emit_aln(0)
        emit_axn(0)
        emit_atrans(0)
        for gi in range(16):
            bi = gi % 2
            if gi + 1 < 16:
                emit_aload(gi + 1)

            for g in range(G):
                pk = psA[g % 2]

                @blk.tensor
                def _(t, g=g, bi=bi, xnT_done=xnT_map[gi], pk=pk, ke=k_evac[g % 2],
                      vready=vready, wkv_done=wkv_done):
                    self.wait(t, "sv", max(xnT_done, vready, wkv_done))
                    self.wait(t, "ss", ke)
                    i = None
                    for ec in range(EC):
                        i = t.matmul(pk[:, 0:256],
                                     wk_sb[:, (g * EC + ec) * 128:(g * EC + ec) * 128 + 128],
                                     xnT[bi][:, ec * 256:(ec + 1) * 256],
                                     start=(ec == 0), stop=(ec == EC - 1))
                    self.inc(i, "st")
                stv = self.cnt["st"]

                @blk.scalar
                def _(s, g=g, gi=gi, pk=pk, stv=stv):
                    self.wait(s, "st", stv)
                    i = s.activation(out=self.K_sb[:, g * S + gi * 256: g * S + gi * 256 + 256],
                                     in_=pk[:, 0:256], func=AF.Identity, bias=self.bk_sb[:, g:g + 1])
                    self.inc(i, "ss")
                k_evac[g % 2] = self.cnt["ss"]

                pv = psB[g % 2]

                @blk.tensor
                def _(t, g=g, bi=bi, pv=pv, ve=v_evac[g % 2]):
                    self.wait(t, "sv", ve)
                    i = None
                    for j in range(2):
                        for ec in range(EC):
                            i = t.matmul(pv[:, j * 128:(j + 1) * 128],
                                         xnT[bi][:, ec * 256 + j * 128: ec * 256 + j * 128 + 128],
                                         wv_sb[:, (g * EC + ec) * 128:(g * EC + ec) * 128 + 128],
                                         start=(ec == 0), stop=(ec == EC - 1))
                    self.inc(i, "st")
                stv = self.cnt["st"]

                @blk.vector
                def _(v, g=g, gi=gi, pv=pv, stv=stv):
                    self.wait(v, "st", stv)
                    dst = self.V_sb[:].rearrange("p (c x) -> p c x", x=VST)
                    bvv = self.flat_ap(self.bvb_sb, g * 128, [[G * 128, 128], [0, 2], [1, 128]])
                    i = v.scalar_tensor_tensor(
                        out=dst[:, g * NTS + gi * 2: g * NTS + gi * 2 + 2, 0:128],
                        in0=pv[:, 0:256].rearrange("p (j d) -> p j d", d=128),
                        scalar=1.0, in1=bvv, op0=ALU.mult, op1=ALU.add)
                    self.inc(i, "sv")
                v_evac[g % 2] = self.cnt["sv"]

                if gi + 1 < 16:
                    if g == 1:
                        emit_aln(gi + 1)
                    elif g == 2:
                        emit_axn(gi + 1)
            if gi + 1 < 16:
                emit_atrans(gi + 1)

            if DEBUG_SERIAL:
                self.serial_barrier()

            if DEBUG_DUMP and gi == 15:
                @blk.gpsimd
                def _(gg, bi=bi, sv=self.cnt["sv"], ss=self.cnt["ss"]):
                    self.wait(gg, "sv", sv)
                    self.wait(gg, "ss", ss)
                    i = gg.dma_start(out=self.dbg_a[:, 0:4096], in_=xnb[bi][:])
                    self.inc(i, "dmk", 16)
                    i = gg.dma_start(out=self.dbg_a[:, 4096:4096 + 4096],
                                     in_=xnT[bi][:])
                    self.inc(i, "dmk", 16)
                    i = gg.dma_start(out=self.dbg_a[:, 2 * 4096:2 * 4096 + 4096], in_=xg[bi][:])
                    self.inc(i, "dmk", 16)
                    i = gg.dma_start(out=self.dbg_a[:, 3 * 4096:3 * 4096 + 64], in_=self.stat[:])
                    self.inc(i, "dmk", 16)
                    gg.wait_ge(self.sems["dmk"], self.cnt["dmk"])

        a1.close()
        if STOP_AFTER == "A1":
            return
        a1_end = dict(st=self.cnt["st"], ss=self.cnt["ss"], sv=self.cnt["sv"])

        @blk.sync
        def _(sy):
            self.wait(sy, "st", a1_end["st"])
            self.wait(sy, "ss", a1_end["ss"])
            self.wait(sy, "sv", a1_end["sv"])

        self.Q_sb = self.sb(r1, "Q_sb", [128, NSLOT * 2048], BF16)
        a2 = ExitStack()
        self.ptp = [self.ps(a2, f"ptpA2{i}", [128, 512], BF16) for i in range(2)]
        psA = [self.ps(a2, f"psA2{i}", [128, 512]) for i in range(2)]
        self._tp_guard = {}
        wq_b = [self.sb(a2, f"wq_b{i}", [128, 2048], BF16) for i in range(2)]
        xo = [self.sb(a2, f"xo{i}", [128, 2048], F32) for i in range(2)]
        xob = [self.sb(a2, f"xob{i}", [128, 2048], BF16) for i in range(2)]
        xnT2 = self.sb(a2, "xnT2", [128, EC * 512], BF16)

        wq_cons = [0, 0]
        wqb_cons = [0, 0]
        xo_cons = [0, 0]
        xob_cons = [0, 0]
        q_evac = [0, 0]
        a1_barrier = dict(ss=self.cnt["ss"], sv=self.cnt["sv"])
        for tg in range(4):
            for jj in range(4):
                tc = tg * 4 + jj
                bi = tc % 2

                @blk.sync
                def _(sy, tc=tc, bi=bi, g=xo_cons[bi]):
                    self.wait(sy, "ss", g)
                    i = sy.dma_start(out=xo[bi][:], in_=self.xown[tc * 128:(tc + 1) * 128, :])
                    self.inc(i, "dxa%d" % bi, 16)
                dd = self.cnt["dxa%d" % bi]

                @blk.vector
                def _(v, bi=bi, dd=dd):
                    self.wait(v, "dxa%d" % bi, dd)
                ln_done = self.ln_stats(xo[bi], 1, ln_guard)

                @blk.scalar
                def _(s, bi=bi, ln_done=ln_done, g=xob_cons[bi]):
                    self.wait(s, "sv", ln_done)
                    self.wait(s, "st", g)
                    i = s.activation(out=xob[bi][:], in_=xo[bi][:], func=AF.Identity,
                                     scale=self.stat[:, 0:1], bias=self.stat[:, 48:49])
                    self.inc(i, "ss")
                xn_done = self.cnt["ss"]
                xo_cons[bi] = xn_done
                ln_guard = xn_done

                @blk.tensor
                def _(t, xn_done=xn_done):
                    self.wait(t, "ss", xn_done)
                for ec4 in range(0, EC, 4):
                    def wr(v, pp, ec4=ec4, jj=jj):
                        dst = xnT2[:].rearrange("p (a q) -> p a q", q=512)
                        return v.tensor_copy(
                            out=dst[:, ec4:ec4 + 4, jj * 128:(jj + 1) * 128],
                            in_=pp[:].rearrange("p (a q) -> p a q", q=128))
                    self.tp4(xob[bi], ec4 * 128, 4, wr)
                xob_cons[bi] = self.cnt["st"]
            xnT2_done = self.cnt["sv"]

            for h in range(H):
                bi = h % 2

                @blk.gpsimd
                def _(gg, tg=tg, h=h, bi=bi, g=wqb_cons[bi]):
                    if tg == 0 and h == 0:
                        self.wait(gg, "st", a1_end["st"])
                        self.wait(gg, "ss", a1_end["ss"])
                        self.wait(gg, "sv", a1_end["sv"])
                    self.wait(gg, "st", g)
                    i = gg.dma_start(out=wq_b[bi][:],
                                     in_=self.wq[h].rearrange("(a p) d -> p a d", p=128))
                    self.inc(i, "dwq%d" % bi, 16)
                cast_done = self.cnt["dwq%d" % bi]

                pq = psA[h % 2]

                @blk.tensor
                def _(t, h=h, tg=tg, bi=bi, pq=pq, cast_done=cast_done,
                      xnT2_done=xnT2_done, qe=q_evac[h % 2], ab=a1_barrier):
                    self.wait(t, "dwq%d" % bi, cast_done)
                    self.wait(t, "sv", xnT2_done)
                    self.wait(t, "ss", max(qe, ab["ss"]))
                    i = None
                    for ec in range(EC):
                        i = t.matmul(pq[:, 0:512], wq_b[bi][:, ec * 128:(ec + 1) * 128],
                                     xnT2[:, ec * 512:(ec + 1) * 512],
                                     start=(ec == 0), stop=(ec == EC - 1))
                    self.inc(i, "st")
                stv = self.cnt["st"]
                wqb_cons[bi] = stv

                @blk.scalar
                def _(s, h=h, tg=tg, pq=pq, stv=stv):
                    self.wait(s, "st", stv)
                    g2, hh = h // 4, h % 4
                    dst = self.Q_sb[:].rearrange("p (t x) -> p t x", x=2048)
                    i = s.activation(
                        out=dst[:, tg * 4:(tg + 1) * 4,
                                g2 * 512 + hh * 128: g2 * 512 + hh * 128 + 128],
                        in_=pq[:].rearrange("p (a q) -> p a q", q=128),
                        func=AF.Identity, bias=self.bq_sb[:, h:h + 1])
                    self.inc(i, "ss")
                q_evac[h % 2] = self.cnt["ss"]
        if DEBUG_DUMP:
            @blk.gpsimd
            def _(gg):
                self.wait(gg, "ss", self.cnt["ss"])
                self.wait(gg, "sv", self.cnt["sv"])
                i = gg.dma_start(out=self.dbg_q, in_=self.Q_sb[:])
                self.inc(i, "dmk", 16)
                i = gg.dma_start(out=self.dbg_k, in_=self.K_sb[:])
                self.inc(i, "dmk", 16)
                i = gg.dma_start(out=self.dbg_v, in_=self.V_sb[:])
                self.inc(i, "dmk", 16)
                gg.wait_ge(self.sems["dmk"], self.cnt["dmk"])
        a2.close()
        self.A_end = dict(ss=self.cnt["ss"], sv=self.cnt["sv"], st=self.cnt["st"])

    def phase_B(self, r1):
        blk = self.block
        b1 = ExitStack()
        mk_sb = self.sb(b1, "mk_sb", [128, NSLOT * 2 * 128], F32)
        pT = [self.sb(b1, f"pT{i}", [128, 1024], BF16) for i in range(2)]
        attn_sb = [self.sb(b1, f"attn_sb{i}", [128, 2048], F32) for i in range(2)]
        linv = self.sb(b1, "linv", [128, 8], F32)
        stp = [self.ps(b1, f"stp{i}", [128, 1024]) for i in range(2)]
        acc = [self.ps(b1, f"acc{i}", [128, 512]) for i in range(4)]

        @blk.sync
        def _(sy):
            self.wait(sy, "st", self.A_end["st"])
            self.wait(sy, "ss", self.A_end["ss"])
            self.wait(sy, "sv", self.A_end["sv"])
            i = sy.dma_start(out=mk_sb[:], in_=self.mk)
            self.inc(i, "dmk", 16)
        mk_done = self.cnt["dmk"]

        @blk.scalar
        def _(s):
            s.preload_act_table(AF.Exp)

        stp_exp = [0, 0]
        pT_av = [0, 0]
        attn_dma = [0, 0]
        acc_free = 0
        first = True
        for s_ in range(NSLOT):
            for gp in range(2):
                av_cnt = {}
                exp_cnt = {}
                for j in range(kstar(s_)):
                    pbuf = j % 2
                    KS = kstar(s_)

                    @blk.tensor
                    def _(t, s_=s_, gp=gp, j=j, pbuf=pbuf, first=first,
                          se=stp_exp[pbuf]):
                        if first:
                            self.wait(t, "ss", self.A_end["ss"])
                            self.wait(t, "sv", self.A_end["sv"])
                        self.wait(t, "ss", se)
                        i = None
                        for g2 in range(2):
                            g = gp * 2 + g2
                            i = t.matmul(stp[pbuf][:, g2 * 512:(g2 + 1) * 512],
                                         self.K_sb[:, g * S + j * 128: g * S + j * 128 + 128],
                                         self.Q_sb[:, s_ * 2048 + g * 512: s_ * 2048 + g * 512 + 512],
                                         start=True, stop=True)
                        self.inc(i, "st")
                    first = False
                    sc_done = self.cnt["st"]

                    masked = j >= KS - 2
                    msk_done = 0
                    if masked:
                        @blk.vector
                        def _(v, s_=s_, j=j, pbuf=pbuf, sc_done=sc_done, KS=KS):
                            self.wait(v, "st", sc_done)
                            self.wait(v, "dmk", mk_done)
                            mi = (s_ * 2 + (j - (KS - 2))) * 128
                            src = self.flat_ap(mk_sb, mi,
                                               [[NSLOT * 2 * 128, 128], [0, 8], [1, 128]])
                            i = v.tensor_tensor(
                                out=stp[pbuf][:].rearrange("p (a q) -> p a q", q=128),
                                in0=stp[pbuf][:].rearrange("p (a q) -> p a q", q=128),
                                in1=src, op=ALU.add)
                            self.inc(i, "sv")
                        msk_done = self.cnt["sv"]

                    @blk.scalar
                    def _(s, j=j, pbuf=pbuf, sc_done=sc_done, masked=masked,
                          msk_done=msk_done, pa=pT_av[pbuf]):
                        if masked:
                            self.wait(s, "sv", msk_done)
                        else:
                            self.wait(s, "st", sc_done)
                        self.wait(s, "st", pa)
                        i = s.activation(out=pT[pbuf][:], in_=stp[pbuf][:], func=AF.Exp,
                                         scale=SCL, bias=self.consts[:, 1:2])
                        self.inc(i, "ss")
                    exp_cnt[j] = self.cnt["ss"]
                    stp_exp[pbuf] = exp_cnt[j]

                    @blk.tensor
                    def _(t, s_=s_, gp=gp, j=j, pbuf=pbuf, KS=KS, ec=exp_cnt[j], af=acc_free):
                        self.wait(t, "ss", ec)
                        if j == 0:
                            self.wait(t, "sv", af)
                        vv = self.V_sb[:].rearrange("p (c x) -> p c x", x=VST)
                        i = None
                        for g2 in range(2):
                            g = gp * 2 + g2
                            for hh in range(4):
                                h8 = g2 * 4 + hh
                                i = t.matmul(acc[h8 // 2][:, (h8 % 2) * 136:(h8 % 2) * 136 + 129],
                                             pT[pbuf][:, h8 * 128:(h8 + 1) * 128],
                                             vv[:, g * NTS + j, 0:129],
                                             start=(j == 0 and h8 % 2 == 0),
                                             stop=(j == KS - 1 and h8 % 2 == 1))
                        self.inc(i, "st")
                    av_cnt[j] = self.cnt["st"]
                    pT_av[pbuf] = av_cnt[j]

                if DEBUG_DUMP and s_ == 15 and gp == 1:
                    @blk.vector
                    def _(v, av=av_cnt[kstar(s_) - 1]):
                        self.wait(v, "st", av)
                        v.tensor_copy(out=self.junk_s[:, 0:1024], in_=stp[0][:])
                        i = v.tensor_copy(out=self.junk_s2[:, 0:1024], in_=stp[1][:])
                        self.inc(i, "sv")

                    @blk.vector
                    def _(v, av=av_cnt[kstar(s_) - 1]):
                        v.tensor_copy(out=self.junk_s[:, 1024:2048], in_=self.K_sb[:, 12 * S // 4:12 * S // 4 + 1024])
                        i = v.tensor_copy(out=self.junk_s2[:, 1024:2048], in_=self.Q_sb[:, 15 * 2048:15 * 2048 + 1024])
                        self.inc(i, "sv")

                    @blk.gpsimd
                    def _(gg, sv=self.cnt["sv"], ec=exp_cnt[kstar(s_) - 1]):
                        self.wait(gg, "sv", sv)
                        self.wait(gg, "ss", ec)
                        i = gg.dma_start(out=self.dbg_c[:, 0:1024], in_=self.junk_s[:, 1024:2048])
                        self.inc(i, "dmk", 16)
                        i = gg.dma_start(out=self.dbg_c[:, 1024:2048], in_=self.junk_s2[:, 1024:2048])
                        self.inc(i, "dmk", 16)
                        i = gg.dma_start(out=self.dbg_b[:, 0:1024], in_=self.junk_s[:, 0:1024])
                        self.inc(i, "dmk", 16)
                        i = gg.dma_start(out=self.dbg_b[:, 1024:2048], in_=self.junk_s2[:, 0:1024])
                        self.inc(i, "dmk", 16)
                        i = gg.dma_start(out=self.dbg_b[:, 2048:3072], in_=pT[0][:])
                        self.inc(i, "dmk", 16)
                        i = gg.dma_start(out=self.dbg_b[:, 3072:4096], in_=pT[1][:])
                        self.inc(i, "dmk", 16)

                @blk.vector
                def _(v, s_=s_, gp=gp, av=av_cnt[kstar(s_) - 1], ad=attn_dma[s_ % 2]):
                    if DEBUG_DUMP and s_ == 15 and gp == 1:
                        self.wait(v, "dmk", self.cnt["dmk"])
                    self.wait(v, "st", av)
                    self.wait(v, "datt%d" % (s_ % 2), ad)
                    i = None
                    for h8 in range(8):
                        i = v.reciprocal(out=linv[:, h8:h8 + 1],
                                         in_=acc[h8 // 2][:, (h8 % 2) * 136 + 128:(h8 % 2) * 136 + 129])
                    self.fence(i, v, "sv")
                    for h8 in range(8):
                        g2, hh = h8 // 4, h8 % 4
                        head = (gp * 2 + g2) * 4 + hh
                        i = v.tensor_scalar_mul(
                            out=attn_sb[s_ % 2][:, head * 128:(head + 1) * 128],
                            in0=acc[h8 // 2][:, (h8 % 2) * 136:(h8 % 2) * 136 + 128],
                            scalar1=linv[:, h8:h8 + 1])
                    self.inc(i, "sv")
                acc_free = self.cnt["sv"]
            evac_done = self.cnt["sv"]

            @blk.gpsimd
            def _(gg, s_=s_, evac_done=evac_done):
                self.wait(gg, "sv", evac_done)
                i = gg.dma_start(out=self.attn_s[s_ * 128:(s_ + 1) * 128, :],
                                 in_=attn_sb[s_ % 2][:])
                self.inc(i, "datt%d" % (s_ % 2), 16)
            attn_dma[s_ % 2] = self.cnt["datt%d" % (s_ % 2)]
            if DEBUG_SERIAL_B:
                self.serial_barrier()
        b1.close()
        self.B_end = dict(st=self.cnt["st"], datt0=self.cnt["datt0"],
                          datt1=self.cnt["datt1"])

    def phase_C(self, r2):
        blk = self.block
        c1 = ExitStack()
        xo = [self.sb(c1, f"cxo{i}", [128, 2048], F32) for i in range(2)]
        xmm = [self.sb(c1, f"xmm{i}", [128, 2048], F32) for i in range(2)]
        qx8 = self.sb(c1, "qx8", [128, 2048], FP8)
        qxd = [self.sb(c1, f"qxd{i}", [128, 2048], BF16) for i in range(2)]
        self.ptp = [self.ps(c1, f"ptpC{i}", [128, 512], BF16) for i in range(2)]
        self._tp_guard = {}

        @blk.scalar
        def _(s):
            self.wait(s, "st", self.B_end["st"])
            s.preload_act_table(AF.Sqrt)

        xo_cons = [0, 0]
        lng = [0, 0]
        bankg = [0, 0]
        qxd_tr = [0, 0]
        ln_map = {}
        sc_map = {}
        qd_map = {}

        def emit_c0_load(tc):
            bi = tc % 2

            @blk.sync
            def _(sy, tc=tc, bi=bi, g=xo_cons[bi]):
                if tc == 0:
                    self.wait(sy, "st", self.B_end["st"])
                    self.wait(sy, "datt0", self.B_end["datt0"])
                    self.wait(sy, "datt1", self.B_end["datt1"])
                self.wait(sy, "ss", g)
                i = sy.dma_start(out=xo[bi][:], in_=self.xown[tc * 128:(tc + 1) * 128, :])
                self.inc(i, "dxa%d" % bi, 16)

        def emit_c0_ln(tc):
            bi = tc % 2
            dd = self.cnt["dxa%d" % bi]

            @blk.vector
            def _(v, bi=bi, dd=dd):
                self.wait(v, "dxa%d" % bi, dd)
            ln_map[tc] = self.ln_stats(xo[bi], 1, lng[bi], o=2 + bi, jb=bi)

        def emit_c0_q(tc):
            bi = tc % 2
            base = 4 if bi == 0 else 12

            @blk.scalar
            def _(s, bi=bi, ln_done=ln_map[tc]):
                self.wait(s, "sv", ln_done)
                i = s.activation(out=xmm[bi][:], in_=xo[bi][:], func=AF.Identity,
                                 scale=self.stat[:, 2 + bi:3 + bi],
                                 bias=self.stat[:, 50 + bi:51 + bi])
                self.inc(i, "ss")
            xmm_done = self.cnt["ss"]
            xo_cons[bi] = xmm_done
            lng[bi] = xmm_done

            sc_map[tc] = self.qscales(xmm[bi][:], max(xmm_done, bankg[bi]), base=base)

            @blk.scalar
            def _(s, bi=bi, base=base, sc_done=sc_map[tc], qt=qxd_tr[bi]):
                self.wait(s, "sv", sc_done)
                self.wait(s, "st", qt)
                i = s.activation(out=qx8[:], in_=xmm[bi][:], func=AF.Copy,
                                 scale=self.stat[:, base + 2:base + 3])
                self.fence(i, s, "ss")
                i = s.activation(out=qxd[bi][:], in_=qx8[:], func=AF.Copy,
                                 scale=self.stat[:, base + 3:base + 4])
                self.inc(i, "ss")
            qd_map[tc] = self.cnt["ss"]
            bankg[bi] = self.cnt["ss"]

        def emit_c0_trans(tc):
            bi = tc % 2

            @blk.tensor
            def _(t, qxd_done=qd_map[tc]):
                self.wait(t, "ss", qxd_done)
            for ec4 in range(0, EC, 4):
                def wr(v, pp, ec4=ec4, tc=tc):
                    dst = self.fmT[:].rearrange("p (a q) -> p a q", q=2048)
                    return v.tensor_copy(
                        out=dst[:, ec4:ec4 + 4, tc * 128:(tc + 1) * 128],
                        in_=pp[:].rearrange("p (a q) -> p a q", q=128))
                self.tp4(qxd[bi], ec4 * 128, 4, wr)
            qxd_tr[bi] = self.cnt["st"]

        emit_c0_load(0)
        emit_c0_ln(0)
        emit_c0_q(0)
        for tc in range(NSLOT):
            if tc + 1 < NSLOT:
                emit_c0_load(tc + 1)
                emit_c0_ln(tc + 1)
            emit_c0_trans(tc)
            if tc + 1 < NSLOT:
                emit_c0_q(tc + 1)
        stat_guard = max(bankg)
        qxT_done = self.cnt["sv"]

        @blk.vector
        def _(v):
            i = v.memset(self.tmax[:], 0.0)
            self.inc(i, "sv")
        tmax_init = self.cnt["sv"]

        wgf = self.sb(c1, "wgf", [128, 2 * 2048], F32)
        wuf = self.sb(c1, "wuf", [128, 2 * 2048], F32)
        w8 = self.sb(c1, "w8", [128, 2048], FP8)
        qwT = [self.sb(c1, f"qwT{i}", [128, EC * 256], BF16) for i in range(2)]
        quT = [self.sb(c1, f"quT{i}", [128, EC * 256], BF16) for i in range(2)]
        wdq = self.sb(c1, "wdq", [128, 2048], BF16)
        sg_sb = [self.sb(c1, f"sg_sb{i}", [128, 512], F32) for i in range(2)]
        tsl = self.sb(c1, "tsl", [128, 8 * 256], F32)
        psG = [self.ps(c1, f"psG{i}", [128, 512]) for i in range(2)]
        psU = [self.ps(c1, f"psU{i}", [128, 512]) for i in range(2)]

        @blk.scalar
        def _(s):
            s.preload_act_table(AF.Silu)

        NHS = 2 * NSL
        w_cons = [0, 0]
        tsl_dma_last = 0
        gu_evac = [0, 0]
        wdq_trans = 0
        mm_done_hist = {}
        wdma_cnt = {}
        qd_cnt = {}
        bank_guard = [stat_guard, stat_guard]
        prep_sv = {}

        def emit_wload(p, wi):
            wf, wsrc = (wgf, self.wg) if wi == 0 else (wuf, self.wu)
            wsem = "dwg" if wi == 0 else "dwu"

            @blk.sync
            def _(sy, p=p, wf=wf, wsrc=wsrc, wsem=wsem, wc=w_cons[wi]):
                self.wait(sy, "ss", wc)
                r0 = p * 256
                i = sy.dma_start(
                    out=wf[:].rearrange("p (c e) -> p c e", e=2048),
                    in_=wsrc[r0:r0 + 256, :].rearrange("(c p) e -> p c e", p=128))
                self.inc(i, wsem, 16)
            wdma_cnt[(p, wi)] = self.cnt[wsem]

        def emit_wquant(p, g):
            wi, cc = g // 2, g % 2
            wf = wgf if wi == 0 else wuf
            wsem = "dwg" if wi == 0 else "dwu"
            base = 17 if g % 2 == 0 else 24
            sc_done = self.qscales(wf[:, cc * 2048:(cc + 1) * 2048],
                                   bank_guard[g % 2], wsem, wdma_cnt[(p, wi)],
                                   base=base)

            @blk.scalar
            def _(s, wf=wf, cc=cc, sc_done=sc_done, wt=wdq_trans, base=base):
                self.wait(s, "sv", sc_done)
                self.wait(s, "st", wt)
                i = s.activation(out=w8[:], in_=wf[:, cc * 2048:(cc + 1) * 2048],
                                 func=AF.Copy, scale=self.stat[:, base + 2:base + 3])
                self.fence(i, s, "ss")
                i = s.activation(out=wdq[:], in_=w8[:], func=AF.Copy,
                                 scale=self.stat[:, base + 3:base + 4])
                self.inc(i, "ss")
            qd_cnt[(p, g)] = self.cnt["ss"]
            bank_guard[g % 2] = self.cnt["ss"]
            if cc == 1:
                w_cons[wi] = self.cnt["ss"]

        def emit_wtrans(p, g):
            nonlocal wdq_trans
            wi, cc = g // 2, g % 2
            wT2 = qwT if wi == 0 else quT
            buf = p % 2

            @blk.tensor
            def _(t, qd=qd_cnt[(p, g)]):
                self.wait(t, "ss", qd)
            mmg = mm_done_hist.get(p - 2, 0)
            for ec4 in range(0, EC, 4):
                def wr(v, pp, wT2=wT2, buf=buf, cc=cc, ec4=ec4, mmg=mmg):
                    self.wait(v, "st", mmg)
                    dst = wT2[buf][:].rearrange("p (a q) -> p a q", q=256)
                    return v.tensor_copy(
                        out=dst[:, ec4:ec4 + 4, cc * 128:(cc + 1) * 128],
                        in_=pp[:].rearrange("p (a q) -> p a q", q=128))
                self.tp4(wdq, ec4 * 128, 4, wr)
            wdq_trans = self.cnt["st"]
            if g == 3:
                prep_sv[p] = self.cnt["sv"]

        emit_wload(0, 0)
        emit_wload(0, 1)
        for g in range(4):
            emit_wquant(0, g)
            emit_wtrans(0, g)
        emit_wload(1, 0)
        emit_wload(1, 1)
        for hs in range(NHS):
            buf = hs % 2
            p = hs + 1
            if p < NHS:
                emit_wquant(p, 0)
            wq_done = prep_sv[hs]

            for tc in range(NSLOT):
                pg, pu = psG[tc % 2], psU[tc % 2]
                slot = tc % 8

                @blk.tensor
                def _(t, hs=hs, tc=tc, buf=buf, pg=pg, pu=pu, wq_done=wq_done,
                      ge=gu_evac[tc % 2], qxT_done=qxT_done):
                    if tc == 0:
                        self.wait(t, "sv", max(wq_done, qxT_done))
                    self.wait(t, "sv", ge)
                    for ec in range(EC):
                        t.matmul(pg[:, 0:256],
                                 self.fmT[:, ec * 2048 + tc * 128: ec * 2048 + tc * 128 + 128],
                                 qwT[buf][:, ec * 256:(ec + 1) * 256],
                                 start=(ec == 0), stop=(ec == EC - 1))
                    i = None
                    for ec in range(EC):
                        i = t.matmul(pu[:, 0:256],
                                     self.fmT[:, ec * 2048 + tc * 128: ec * 2048 + tc * 128 + 128],
                                     quT[buf][:, ec * 256:(ec + 1) * 256],
                                     start=(ec == 0), stop=(ec == EC - 1))
                    self.inc(i, "st")
                gu_done = self.cnt["st"]

                if p < NHS and tc in (2, 5, 8, 11):
                    g = (2, 5, 8, 11).index(tc)
                    emit_wtrans(p, g)
                    if g < 3:
                        emit_wquant(p, g + 1)
                    elif p + 1 < NHS:
                        emit_wload(p + 1, 1)
                if p + 1 < NHS and tc == 5:
                    emit_wload(p + 1, 0)

                @blk.scalar
                def _(s, tc=tc, pg=pg, gu_done=gu_done, ge=gu_evac[tc % 2]):
                    self.wait(s, "st", gu_done)
                    self.wait(s, "sv", ge)
                    i = s.activation(out=sg_sb[tc % 2][:, 0:256], in_=pg[:, 0:256], func=AF.Silu)
                    self.inc(i, "ss")
                sg_done = self.cnt["ss"]

                @blk.vector
                def _(v, hs=hs, tc=tc, pu=pu, slot=slot, sg_done=sg_done,
                      td=tsl_dma_last, tmax_init=tmax_init):
                    self.wait(v, "ss", sg_done)
                    self.wait(v, "sv", tmax_init)
                    self.wait(v, "dtsl", td)
                    i = v.tensor_mul(out=tsl[:, slot * 256:(slot + 1) * 256],
                                     in0=sg_sb[tc % 2][:, 0:256], in1=pu[:, 0:256])
                    self.fence(i, v, "sv")
                    i = v.reduce_max(out=self.stat[:, 22:23], in_=tsl[:, slot * 256:(slot + 1) * 256],
                                     axis=AX.X, apply_absolute_value=True)
                    self.fence(i, v, "sv")
                    i = v.tensor_max(out=self.tmax[:, tc:tc + 1], in0=self.tmax[:, tc:tc + 1],
                                     in1=self.stat[:, 22:23])
                    self.inc(i, "sv")
                gu_evac[tc % 2] = self.cnt["sv"]

                if tc % 8 == 7:
                    tch = tc // 8
                    t_done = self.cnt["sv"]

                    @blk.gpsimd
                    def _(gg, hs=hs, tch=tch, t_done=t_done):
                        self.wait(gg, "sv", t_done)
                        dst = self.t_s[:, hs * 256:(hs + 1) * 256]\
                            .rearrange("(t p) i -> p t i", p=128)
                        i = gg.dma_start(out=dst[:, tch * 8:(tch + 1) * 8, :],
                                         in_=tsl[:].rearrange("p (t i) -> p t i", i=256))
                        self.inc(i, "dtsl", 16)
                    tsl_dma_last = self.cnt["dtsl"]
            mm_done_hist[hs] = self.cnt["st"]
        stat_guard = max(bank_guard)
        if DEBUG_DUMP:
            @blk.gpsimd
            def _(gg):
                for n in ("st", "ss", "sv"):
                    self.wait(gg, n, self.cnt[n])
                i = gg.dma_start(out=self.dbg_d[:, 0:EC * 2048], in_=self.fmT[:])
                self.inc(i, "dmk", 16)
                i = gg.dma_start(out=self.dbg_d[:, EC * 2048:EC * 2048 + EC * 256], in_=qwT[0][:])
                self.inc(i, "dmk", 16)
                i = gg.dma_start(out=self.dbg_d[:, EC * 2048 + EC * 512:EC * 2048 + EC * 512 + EC * 256],
                                 in_=quT[0][:])
                self.inc(i, "dmk", 16)
                i = gg.dma_start(out=self.dbg_d[:, EC * 2048 + 2 * EC * 512:EC * 2048 + 2 * EC * 512 + 64],
                                 in_=self.stat[:])
                self.inc(i, "dmk", 16)
                i = gg.dma_start(out=self.dbg_d[:, EC * 2048 + 2 * EC * 512 + 64:],
                                 in_=self.tmax[:])
                self.inc(i, "dmk", 16)
                gg.wait_ge(self.sems["dmk"], self.cnt["dmk"])
        c1.close()
        self.C_end = dict(st=self.cnt["st"], dtsl=self.cnt["dtsl"], ss=self.cnt["ss"])
        self._c_stat_guard = stat_guard

    def phase_D(self, r3):
        blk = self.block
        d1 = ExitStack()
        qwdT = self.sb(d1, "qwdT", [128, ICH * 1024], BF16)
        wdf = self.sb(d1, "wdf", [128, II], F32)
        w8d = self.sb(d1, "w8d", [128, II], FP8)
        wddq = self.sb(d1, "wddq", [128, II], BF16)
        tt = self.sb(d1, "tt", [128, II], F32)
        qt8 = w8d
        qtd = self.sb(d1, "qtd", [128, II], BF16)
        qtT = [self.sb(d1, f"qtT{i}", [128, ICH * 128], BF16) for i in range(2)]
        ysb = self.sb(d1, "ysb", [128, 1024], F32)
        self.ptp = [self.ps(d1, f"ptpD{i}", [128, 512], BF16) for i in range(2)]
        self._tp_guard = {}
        psY = [self.ps(d1, f"psY{i}", [128, 512]) for i in range(2)]

        stat_guard = self._c_stat_guard
        mm_hist = {}
        wdf_cons = 0
        wddq_trans = 0
        tt_cons = 0
        qtd_trans = 0
        y_evac = 0
        y_dma = [0, 0]
        wdma_map = {}
        wqd_map = {}
        qwd_done = {}

        def emit_wdload(oh, occ):
            @blk.sync
            def _(sy, oh=oh, occ=occ, wc=wdf_cons):
                if oh == 0 and occ == 0:
                    self.wait(sy, "st", self.C_end["st"])
                    self.wait(sy, "dtsl", self.C_end["dtsl"])
                self.wait(sy, "ss", wc)
                r0 = oh * 1024 + occ * 128
                i = sy.dma_start(out=wdf[:], in_=self.wd[r0:r0 + 128, :])
                self.inc(i, "dwd", 16)
            wdma_map[(oh, occ)] = self.cnt["dwd"]

        def emit_wdquant(oh, occ):
            nonlocal stat_guard, wdf_cons
            sc_done = self.qscales(wdf[:], stat_guard, "dwd", wdma_map[(oh, occ)])

            @blk.scalar
            def _(s, sc_done=sc_done, wt=wddq_trans):
                self.wait(s, "sv", sc_done)
                self.wait(s, "st", wt)
                i = s.activation(out=w8d[:], in_=wdf[:], func=AF.Copy,
                                 scale=self.stat[:, 19:20])
                self.fence(i, s, "ss")
                i = s.activation(out=wddq[:], in_=w8d[:], func=AF.Copy,
                                 scale=self.stat[:, 20:21])
                self.inc(i, "ss")
            wdf_cons = self.cnt["ss"]
            stat_guard = self.cnt["ss"]
            wqd_map[(oh, occ)] = self.cnt["ss"]

        def emit_wdtrans(oh, occ):
            nonlocal wddq_trans

            @blk.tensor
            def _(t, qd=wqd_map[(oh, occ)]):
                self.wait(t, "ss", qd)
            for ic4 in range(0, ICH, 4):
                nn = min(4, ICH - ic4)

                def wr(v, pp, occ=occ, ic4=ic4, nn=nn):
                    dst = qwdT[:].rearrange("p (a q) -> p a q", q=1024)
                    return v.tensor_copy(
                        out=dst[:, ic4:ic4 + nn, occ * 128:(occ + 1) * 128],
                        in_=pp[:, 0:nn * 128].rearrange("p (a q) -> p a q", q=128))
                self.tp4(wddq, ic4 * 128, nn, wr)
            wddq_trans = self.cnt["st"]
            if occ == 7:
                qwd_done[oh] = self.cnt["sv"]

        def emit_dtc_loop(oh):
            nonlocal tt_cons, qtd_trans, y_evac
            tprep_sv = {}
            qtd_done_map = {}

            def emit_tload_quant(tc):
                nonlocal tt_cons

                @blk.gpsimd
                def _(gg, oh=oh, tc=tc, g=tt_cons):
                    self.wait(gg, "ss", g)
                    i = gg.dma_start(out=tt[:], in_=self.t_s[tc * 128:(tc + 1) * 128, :])
                    self.inc(i, "dtt", 16)
                tdma = self.cnt["dtt"]

                @blk.vector
                def _(v, tc=tc, tdma=tdma, sg=self._stat_guard):
                    self.wait(v, "dtt", tdma)
                    self.wait(v, "ss", sg)
                    i = v.reciprocal(out=self.stat[:, 25:26], in_=self.tmax[:, tc:tc + 1])
                    self.fence(i, v, "sv")
                    v.tensor_scalar_mul(out=self.stat[:, 26:27], in0=self.stat[:, 25:26],
                                        scalar1=224.0)
                    i = v.tensor_scalar_mul(out=self.stat[:, 27:28], in0=self.tmax[:, tc:tc + 1],
                                            scalar1=2.0 / 448.0)
                    self.inc(i, "sv")
                tsc_done = self.cnt["sv"]

                @blk.scalar
                def _(s, tsc_done=tsc_done, qt=qtd_trans):
                    self.wait(s, "sv", tsc_done)
                    self.wait(s, "st", qt)
                    i = s.activation(out=qt8[:], in_=tt[:], func=AF.Copy,
                                     scale=self.stat[:, 26:27])
                    self.fence(i, s, "ss")
                    i = s.activation(out=qtd[:], in_=qt8[:], func=AF.Copy,
                                     scale=self.stat[:, 27:28])
                    self.inc(i, "ss")
                tt_cons = self.cnt["ss"]
                self._stat_guard = self.cnt["ss"]
                qtd_done_map[tc] = self.cnt["ss"]

            def emit_ttrans(tc):
                nonlocal qtd_trans
                bi = tc % 2

                @blk.tensor
                def _(t, qtd_done=qtd_done_map[tc]):
                    self.wait(t, "ss", qtd_done)
                mmg = mm_hist.get(tc - 2 + oh * NSLOT, 0)
                for ic4 in range(0, ICH, 4):
                    nn = min(4, ICH - ic4)

                    def wr(v, pp, bi=bi, ic4=ic4, nn=nn, mmg=mmg):
                        self.wait(v, "st", mmg)
                        return v.tensor_copy(out=qtT[bi][:, ic4 * 128:(ic4 + nn) * 128],
                                             in_=pp[:, 0:nn * 128])
                    self.tp4(qtd, ic4 * 128, nn, wr)
                qtd_trans = self.cnt["st"]
                tprep_sv[tc] = self.cnt["sv"]

            emit_tload_quant(0)
            emit_ttrans(0)
            for tc in range(NSLOT):
                bi = tc % 2
                if tc + 1 < NSLOT:
                    emit_tload_quant(tc + 1)
                qtT_done = tprep_sv[tc]

                @blk.tensor
                def _(t, bi=bi, qtT_done=qtT_done, qd=qwd_done[oh], ye=y_evac):
                    self.wait(t, "sv", max(qtT_done, qd))
                    self.wait(t, "sv", ye)
                    for ic in range(ICH):
                        t.matmul(psY[0][:], qtT[bi][:, ic * 128:(ic + 1) * 128],
                                 qwdT[:, ic * 1024: ic * 1024 + 512],
                                 start=(ic == 0), stop=(ic == ICH - 1))
                    i = None
                    for ic in range(ICH):
                        i = t.matmul(psY[1][:], qtT[bi][:, ic * 128:(ic + 1) * 128],
                                     qwdT[:, ic * 1024 + 512: ic * 1024 + 1024],
                                     start=(ic == 0), stop=(ic == ICH - 1))
                    self.inc(i, "st")
                mm_done = self.cnt["st"]
                mm_hist[tc + oh * NSLOT] = mm_done

                if tc + 1 < NSLOT:
                    emit_ttrans(tc + 1)

                @blk.vector
                def _(v, mm_done=mm_done, yd=y_dma[0]):
                    self.wait(v, "st", mm_done)
                    self.wait(v, "dy0", yd)
                    v.tensor_copy(out=ysb[:, 0:512], in_=psY[0][:])
                    i = v.tensor_copy(out=ysb[:, 512:1024], in_=psY[1][:])
                    self.inc(i, "sv")
                y_evac = self.cnt["sv"]
                yv = self.cnt["sv"]

                @blk.gpsimd
                def _(gg, oh=oh, tc=tc, yv=yv):
                    self.wait(gg, "sv", yv)
                    i = gg.dma_start(
                        out=self.y_s[tc * 128:(tc + 1) * 128, oh * 1024:(oh + 1) * 1024],
                        in_=ysb[:])
                    self.inc(i, "dy0", 16)
                y_dma[0] = self.cnt["dy0"]

        for oh in range(2):
            emit_wdload(oh, 0)
            for occ in range(8):
                emit_wdquant(oh, occ)
                if occ + 1 < 8:
                    emit_wdload(oh, occ + 1)
                emit_wdtrans(oh, occ)
            self._stat_guard = stat_guard
            emit_dtc_loop(oh)
        d1.close()
        self.D_end = dict(dy0=self.cnt["dy0"], dy1=self.cnt["dy1"],
                          sv=self.cnt["sv"], ss=self.cnt["ss"], st=self.cnt["st"],
                          dwd=self.cnt["dwd"], dtt=self.cnt["dtt"])

    def phase_F(self, r4):
        blk = self.block
        f1 = ExitStack()
        fx = [self.sb(f1, f"fx{i}", [128, 2048], F32) for i in range(2)]
        fa = [self.sb(f1, f"fa{i}", [128, 2048], F32) for i in range(2)]
        fy = [self.sb(f1, f"fy{i}", [128, 2048], F32) for i in range(2)]
        fo = [self.sb(f1, f"fo{i}", [128, 2048], F32) for i in range(2)]
        f_cons = [0, 0]
        out_dma = [0, 0]
        for tc in range(NSLOT):
            bi = tc % 2

            @blk.sync
            def _(sy, tc=tc, bi=bi, g=f_cons[bi]):
                if tc == 0:
                    for n in ("sv", "ss", "st", "dtt"):
                        self.wait(sy, n, self.D_end[n])
                self.wait(sy, "sv", g)
                i = sy.dma_start(out=fx[bi][:], in_=self.xown[tc * 128:(tc + 1) * 128, :])
                self.inc(i, "dfx%d" % bi, 16)

            @blk.scalar
            def _(s, tc=tc, bi=bi, g=f_cons[bi]):
                if tc == 0:
                    self.wait(s, "datt0", self.B_end["datt0"])
                    self.wait(s, "datt1", self.B_end["datt1"])
                    for n in ("sv", "ss", "st", "dwd", "dtt"):
                        self.wait(s, n, self.D_end[n])
                self.wait(s, "sv", g)
                i = s.dma_start(out=fa[bi][:], in_=self.attn_s[tc * 128:(tc + 1) * 128, :])
                self.inc(i, "dfa%d" % bi, 16)

            @blk.gpsimd
            def _(gg, tc=tc, bi=bi, g=f_cons[bi]):
                if tc == 0:
                    for n in ("sv", "ss", "st", "dwd", "dy0", "dy1"):
                        self.wait(gg, n, self.D_end[n])
                self.wait(gg, "sv", g)
                i = gg.dma_start(out=fy[bi][:], in_=self.y_s[tc * 128:(tc + 1) * 128, :])
                self.inc(i, "dfy%d" % bi, 16)

            fdx = self.cnt["dfx%d" % bi]
            fda = self.cnt["dfa%d" % bi]
            fdy = self.cnt["dfy%d" % bi]

            @blk.vector
            def _(v, tc=tc, bi=bi, fdx=fdx, fda=fda, fdy=fdy, od=out_dma[bi]):
                self.wait(v, "dfx%d" % bi, fdx)
                self.wait(v, "dfa%d" % bi, fda)
                self.wait(v, "dfy%d" % bi, fdy)
                self.wait(v, "dfo%d" % bi, od)
                i = v.tensor_add(out=fo[bi][:], in0=fx[bi][:], in1=fa[bi][:])
                self.fence(i, v, "sv")
                i = v.tensor_add(out=fo[bi][:], in0=fo[bi][:], in1=fy[bi][:])
                self.inc(i, "sv")
            f_cons[bi] = self.cnt["sv"]
            add_done = self.cnt["sv"]

            if bi == 0:
                @blk.sync
                def _(sy, tc=tc, bi=bi, add_done=add_done):
                    self.wait(sy, "sv", add_done)
                    i = sy.dma_start(out=self.out[tc * 128:(tc + 1) * 128, :], in_=fo[bi][:])
                    self.inc(i, "dfo%d" % bi, 16)
            else:
                @blk.scalar
                def _(s, tc=tc, bi=bi, add_done=add_done):
                    self.wait(s, "sv", add_done)
                    i = s.dma_start(out=self.out[tc * 128:(tc + 1) * 128, :], in_=fo[bi][:])
                    self.inc(i, "dfo%d" % bi, 16)
            out_dma[bi] = self.cnt["dfo%d" % bi]
        f1.close()

        @blk.gpsimd
        def _(gg):
            gg.wait_ge(self.sems["dfo0"], self.cnt["dfo0"])
            gg.wait_ge(self.sems["dfo1"], self.cnt["dfo1"])


def build_program():
    return Prog().build()


_CACHE = {}


def _get_nc():
    if "nc" not in _CACHE:
        _CACHE["nc"] = build_program()
    return _CACHE["nc"]


def _tri_mask():
    kl = np.arange(128)[:, None]
    ql = np.arange(128)[None, :]
    return np.where(kl > ql, np.float32(NEG), np.float32(0.0)).astype(np.float32)


def _masks_for(h):
    m = np.zeros((NSLOT, 2, 128, 128), dtype=np.float32)
    tri = _tri_mask()
    for s in range(NSLOT):
        exact = (s >= 8) if h == 0 else (s < 8)
        if exact:
            m[s, 0] = 0.0
            m[s, 1] = tri
        else:
            m[s, 0] = tri
            m[s, 1] = NEG
    return np.ascontiguousarray(m.transpose(2, 0, 1, 3).reshape(128, NSLOT * 2 * 128))


def _own_tiles(h):
    return [2 * s + h for s in range(8)] + [2 * s + 1 - h for s in range(8, 16)]


def kernel(L0_attn_x, L0_attn_Wq, L0_attn_bq, L0_attn_Wk, L0_attn_bk, L0_attn_Wv,
           L0_attn_bv, L0_ffn_w_gate, L0_ffn_w_up, L0_ffn_w_down,
           L0_ffn_up_reorder_index, L0_ffn_down_reorder_index):
    nc = _get_nc()
    x = np.asarray(L0_attn_x, dtype=np.float32)
    wq = np.ascontiguousarray(np.asarray(L0_attn_Wq, dtype=np.float32))
    bq = np.ascontiguousarray(np.asarray(L0_attn_bq, dtype=np.float32))
    wk = np.ascontiguousarray(np.asarray(L0_attn_Wk, dtype=np.float32))
    bk = np.ascontiguousarray(np.asarray(L0_attn_bk, dtype=np.float32))
    wv = np.ascontiguousarray(np.asarray(L0_attn_Wv, dtype=np.float32))
    bv = np.ascontiguousarray(np.asarray(L0_attn_bv, dtype=np.float32))
    dperm = np.asarray(L0_ffn_down_reorder_index).astype(np.int64)
    wg = np.ascontiguousarray(np.asarray(L0_ffn_w_gate, dtype=np.float32)[dperm])
    wu = np.ascontiguousarray(np.asarray(L0_ffn_w_up, dtype=np.float32)[dperm])
    wd = np.ascontiguousarray(np.asarray(L0_ffn_w_down, dtype=np.float32))
    bvb = np.ascontiguousarray(np.broadcast_to(bv[:, None, :], (G, 128, D))).astype(np.float32)
    iden = np.eye(128, dtype=ml_dtypes.bfloat16)

    in_maps = []
    for c in range(8):
        b, h = c // 2, c % 2
        xs = np.ascontiguousarray(x[b])
        rows = np.concatenate([np.arange(t * 128, (t + 1) * 128) for t in _own_tiles(h)])
        xo = np.ascontiguousarray(xs[rows])
        in_maps.append({
            "xseq": xs, "xown": xo, "wq": wq, "bq": np.ascontiguousarray(bq.T),
            "wk": wk, "bk": np.ascontiguousarray(bk.T),
            "wv": wv, "bvb": bvb, "wg": wg, "wu": wu, "wd": wd,
            "mk": _masks_for(h), "iden": iden,
        })
    res = run_bass_kernel_spmd(nc, in_maps, core_ids=list(range(8)))
    out = np.empty((B, S, E), dtype=np.float32)
    for c in range(8):
        b, h = c // 2, c % 2
        oc = np.asarray(res.results[c]["out"])
        for si, t in enumerate(_own_tiles(h)):
            out[b, t * 128:(t + 1) * 128, :] = oc[si * 128:(si + 1) * 128, :]
    return out


# version 13 (final)
# speedup vs baseline: 1.0038x; 1.0038x over previous
import math
from contextlib import ExitStack

import numpy as np
import ml_dtypes

import concourse.bass as bass
import concourse.mybir as mybir
from concourse.bass_utils import run_bass_kernel_spmd

F32 = mybir.dt.float32
BF16 = mybir.dt.bfloat16
FP8 = mybir.dt.float8e4
AF = mybir.ActivationFunctionType
ALU = mybir.AluOpType
AX = mybir.AxisListType

B, S, E, H, G, D = 4, 4096, 2048, 16, 4, 128
II = 5632
NTS = S // 128
NSLOT = 16
EC = E // 128
ICH = II // 128
NSL = 11
SCL = 1.0 / math.sqrt(D)
CSHIFT = 20.0
EPS = 1e-5
VST = 136
NEG = -1.0e30


def kstar(s):
    return 2 * s + 2


DEBUG_DUMP = False
DEBUG_SERIAL = False
DEBUG_SERIAL_B = False
DEBUG_SERIAL_C = False
STOP_AFTER = ""


class Prog:
    def __init__(self):
        nc = bass.Bass("TRN2", target_bir_lowering=False, detect_race_conditions=True)
        self.nc = nc
        d = nc.dram_tensor
        self.xseq = d("xseq", [S, E], F32, kind="ExternalInput").ap()
        self.xown = d("xown", [NSLOT * 128, E], F32, kind="ExternalInput").ap()
        self.wq = d("wq", [H, E, D], F32, kind="ExternalInput").ap()
        self.bq = d("bq", [D, H], F32, kind="ExternalInput").ap()
        self.wk = d("wk", [G, E, D], F32, kind="ExternalInput").ap()
        self.bk = d("bk", [D, G], F32, kind="ExternalInput").ap()
        self.wv = d("wv", [G, E, D], F32, kind="ExternalInput").ap()
        self.bvb = d("bvb", [G, 128, D], F32, kind="ExternalInput").ap()
        self.wg = d("wg", [II, E], F32, kind="ExternalInput").ap()
        self.wu = d("wu", [II, E], F32, kind="ExternalInput").ap()
        self.wd = d("wd", [E, II], F32, kind="ExternalInput").ap()
        self.mk = d("mk", [128, NSLOT * 2 * 128], F32, kind="ExternalInput").ap()
        self.iden = d("iden", [128, 128], BF16, kind="ExternalInput").ap()
        sk = "ExternalOutput" if DEBUG_DUMP else "Internal"
        self.attn_s = d("attn_s", [NSLOT * 128, E], F32, kind=sk).ap()
        self.t_s = d("t_s", [NSLOT * 128, II], F32, kind=sk).ap()
        self.y_s = d("y_s", [NSLOT * 128, E], F32, kind=sk).ap()
        self.out = d("out", [NSLOT * 128, E], F32, kind="ExternalOutput").ap()
        if DEBUG_DUMP:
            self.dbg_q = d("dbg_q", [128, NSLOT * 2048], F32, kind="ExternalOutput").ap()
            self.dbg_k = d("dbg_k", [128, G * S], F32, kind="ExternalOutput").ap()
            self.dbg_v = d("dbg_v", [128, G * NTS * VST], F32, kind="ExternalOutput").ap()
            self.dbg_a = d("dbg_a", [128, 3 * 4096 + 64], F32, kind="ExternalOutput").ap()
            self.dbg_b = d("dbg_b", [128, 4096], F32, kind="ExternalOutput").ap()
            self.dbg_c = d("dbg_c", [128, 2048], F32, kind="ExternalOutput").ap()
            self.dbg_d = d("dbg_d", [128, EC * 2048 + 2 * EC * 512 + 64 + NSLOT], F32,
                           kind="ExternalOutput").ap()

        self.es = ExitStack()
        self.sems = {}
        self.cnt = {}
        for n in ("sv", "ss", "st", "dini", "dwkv", "dxa0", "dxa1", "dwq0", "dwq1",
                  "dmk", "datt0", "datt1", "dwg", "dwu", "dtsl", "dwd", "dtt",
                  "dy0", "dy1", "dfx0", "dfx1", "dfo0", "dfo1",
                  "dfa0", "dfa1", "dfy0", "dfy1"):
            self.sems[n] = self.es.enter_context(nc.semaphore(n))
            self.cnt[n] = 0

    def sb(self, es, name, shape, dtype):
        return es.enter_context(self.nc.sbuf_tensor(name, shape, dtype))

    def ps(self, es, name, shape, dtype=F32):
        return es.enter_context(self.nc.psum_tensor(name, shape, dtype))

    def inc(self, inst, name, amt=1):
        inst.then_inc(self.sems[name], amt)
        self.cnt[name] += amt
        return self.cnt[name]

    def wait(self, eng, name, val):
        if val > 0:
            eng.wait_ge(self.sems[name], val)

    def serial_barrier(self):
        blk = self.block
        snap = dict(self.cnt)
        for ename in ("sync", "vector", "scalar", "tensor", "gpsimd"):
            def f(eng, snap=snap):
                for n, v in snap.items():
                    self.wait(eng, n, v)
            getattr(blk, ename)(f)

    def fence(self, inst, eng, name):
        v = self.inc(inst, name)
        eng.wait_ge(self.sems[name], v)
        return v

    def flat_ap(self, handle, off, pairs):
        return bass.AP(handle, off, pairs)

    def tp4(self, src_sb, c0, n, writer, stride=128):
        blk = self.block
        key = self._tp_idx % 2
        self._tp_idx += 1
        pp = self.ptp[key]
        guard = self._tp_guard.get(key, 0)

        @blk.tensor
        def _(t):
            self.wait(t, "sv", guard)
            i = None
            for q in range(n):
                i = t.transpose(pp[:, q * 128:(q + 1) * 128],
                                src_sb[:, c0 + q * stride: c0 + q * stride + 128], self.ident[:])
            self.inc(i, "st")
        stv = self.cnt["st"]

        @blk.vector
        def _(v):
            self.wait(v, "st", stv)
            i = writer(v, pp)
            self._tp_guard[key] = self.inc(i, "sv")

    def tp4list(self, src_sb, cols, writer):
        blk = self.block
        key = self._tp_idx % 2
        self._tp_idx += 1
        pp = self.ptp[key]
        guard = self._tp_guard.get(key, 0)

        @blk.tensor
        def _(t):
            self.wait(t, "sv", guard)
            i = None
            for q, c in enumerate(cols):
                i = t.transpose(pp[:, q * 128:(q + 1) * 128],
                                src_sb[:, c:c + 128], self.ident[:])
            self.inc(i, "st")
        stv = self.cnt["st"]

        @blk.vector
        def _(v):
            self.wait(v, "st", stv)
            i = writer(v, pp)
            self._tp_guard[key] = self.inc(i, "sv")

    def ln_stats(self, xg, ntile, prev_guard, o=0, jb=0):
        blk, stat = self.block, self.stat

        @blk.vector
        def _(v):
            self.wait(v, "ss", prev_guard)
            i = None
            for j in range(ntile):
                i = v.reduce_sum(out=stat[:, 24 + o + j:25 + o + j],
                                 in_=xg[:, j * 2048:(j + 1) * 2048], axis=AX.X)
            self.fence(i, v, "sv")
            i = v.tensor_scalar_mul(out=stat[:, 16 + o:16 + o + ntile],
                                    in0=stat[:, 24 + o:24 + o + ntile],
                                    scalar1=1.0 / 2048.0)
            self.inc(i, "sv")
        v_sum = self.cnt["sv"]

        for j in range(ntile):
            @blk.scalar
            def _(s, j=j):
                if j == 0:
                    self.wait(s, "sv", max(v_sum, self._jsq_free[jb]))
                i = s.activation(out=self.junk_s[:, (jb + j) * 1024:(jb + j + 1) * 1024],
                                 in_=xg[:, j * 2048: j * 2048 + 1024], func=AF.Square)
                i = s.activation(out=self.junk_s2[:, (jb + j) * 1024:(jb + j + 1) * 1024],
                                 in_=xg[:, j * 2048 + 1024: (j + 1) * 2048], func=AF.Square)
                self.inc(i, "ss")
        s_sq = self.cnt["ss"]

        @blk.vector
        def _(v):
            self.wait(v, "ss", s_sq)
            i = None
            for j in range(ntile):
                v.reduce_sum(out=stat[:, 32 + o + j:33 + o + j],
                             in_=self.junk_s[:, (jb + j) * 1024:(jb + j + 1) * 1024], axis=AX.X)
                i = v.reduce_sum(out=stat[:, 40 + o + j:41 + o + j],
                                 in_=self.junk_s2[:, (jb + j) * 1024:(jb + j + 1) * 1024], axis=AX.X)
            self.fence(i, v, "sv")
            i = v.tensor_add(out=stat[:, 32 + o:32 + o + ntile], in0=stat[:, 32 + o:32 + o + ntile],
                             in1=stat[:, 40 + o:40 + o + ntile])
            self._jsq_free[jb] = self.fence(i, v, "sv")
            v.tensor_scalar_mul(out=stat[:, 40 + o:40 + o + ntile],
                                in0=stat[:, 32 + o:32 + o + ntile],
                                scalar1=1.0 / 2048.0)
            i = v.tensor_mul(out=stat[:, 56 + o:56 + o + ntile], in0=stat[:, 16 + o:16 + o + ntile],
                             in1=stat[:, 16 + o:16 + o + ntile])
            self.fence(i, v, "sv")
            i = v.tensor_sub(out=stat[:, 56 + o:56 + o + ntile],
                             in0=stat[:, 40 + o:40 + o + ntile],
                             in1=stat[:, 56 + o:56 + o + ntile])
            self.inc(i, "sv")
        v_var = self.cnt["sv"]

        @blk.scalar
        def _(s):
            self.wait(s, "sv", v_var)
            i = s.activation(out=stat[:, 8 + o:8 + o + ntile], in_=stat[:, 56 + o:56 + o + ntile],
                             func=AF.Sqrt, bias=self.consts[:, 0:1])
            self.inc(i, "ss")
        s_sqrt = self.cnt["ss"]

        @blk.vector
        def _(v):
            self.wait(v, "ss", s_sqrt)
            i = v.reciprocal(out=stat[:, o:o + ntile], in_=stat[:, 8 + o:8 + o + ntile])
            self.fence(i, v, "sv")
            i = v.tensor_mul(out=stat[:, 48 + o:48 + o + ntile], in0=stat[:, 16 + o:16 + o + ntile],
                             in1=stat[:, o:o + ntile])
            self.fence(i, v, "sv")
            i = v.tensor_scalar_mul(out=stat[:, 48 + o:48 + o + ntile],
                                    in0=stat[:, 48 + o:48 + o + ntile], scalar1=-1.0)
            self.inc(i, "sv")
        return self.cnt["sv"]

    def qscales(self, src_ap, prev_guard, wait_kind=None, wait_val=0, base=17):
        blk, stat = self.block, self.stat

        @blk.vector
        def _(v):
            if wait_kind:
                self.wait(v, wait_kind, wait_val)
            self.wait(v, "ss", prev_guard)
            i = v.reduce_max(out=stat[:, base:base + 1], in_=src_ap, axis=AX.X,
                             apply_absolute_value=True)
            self.fence(i, v, "sv")
            i = v.reciprocal(out=stat[:, base + 1:base + 2], in_=stat[:, base:base + 1])
            self.fence(i, v, "sv")
            v.tensor_scalar_mul(out=stat[:, base + 2:base + 3],
                                in0=stat[:, base + 1:base + 2], scalar1=224.0)
            i = v.tensor_scalar_mul(out=stat[:, base + 3:base + 4],
                                    in0=stat[:, base:base + 1], scalar1=2.0 / 448.0)
            self.inc(i, "sv")
        return self.cnt["sv"]

    def build(self):
        nc, es = self.nc, self.es
        self.ident = self.sb(es, "ident", [128, 128], BF16)
        self.bq_sb = self.sb(es, "bq_sb", [128, H], F32)
        self.bk_sb = self.sb(es, "bk_sb", [128, G], F32)
        self.bvb_sb = self.sb(es, "bvb_sb", [128, G * 128], F32)
        self.stat = self.sb(es, "stat", [128, 64], F32)
        self.junk_s = self.sb(es, "junk_s", [128, 2048], F32)
        self.junk_s2 = self.sb(es, "junk_s2", [128, 2048], F32)
        self._jsq_free = [0, 0]
        self.tmax = self.sb(es, "tmax", [128, NSLOT], F32)
        self.consts = self.sb(es, "consts", [128, 2], F32)

        self._tp_idx = 0
        self._tp_guard = {}

        self.block = es.enter_context(nc.Block())
        blk = self.block

        @blk.sync
        def _(sy):
            i = sy.dma_start(out=self.ident[:], in_=self.iden)
            self.inc(i, "dini", 16)
            i = sy.dma_start(out=self.bq_sb[:], in_=self.bq)
            self.inc(i, "dini", 16)
            i = sy.dma_start(out=self.bk_sb[:], in_=self.bk)
            self.inc(i, "dini", 16)
            i = sy.dma_start(out=self.bvb_sb[:].rearrange("p (g d) -> p g d", d=128),
                             in_=self.bvb.rearrange("g p d -> p g d"))
            self.inc(i, "dini", 16)
        self.setup_dx = self.cnt["dini"]

        @blk.vector
        def _(v):
            v.memset(self.consts[:, 0:1], EPS)
            i = v.memset(self.consts[:, 1:2], -CSHIFT)
            self.inc(i, "sv")
        self.consts_sv = self.cnt["sv"]

        @blk.scalar
        def _(s):
            s.preload_act_table(AF.Sqrt)

        r1 = ExitStack()
        self.K_sb = self.sb(r1, "K_sb", [128, G * S], BF16)
        self.V_sb = self.sb(r1, "V_sb", [128, G * NTS * VST], BF16)
        self.phase_A(r1)
        if not STOP_AFTER.startswith("A"):
            self.phase_B(r1)
        r1.close()
        if not (STOP_AFTER.startswith("A") or STOP_AFTER == "B"):
            r2 = ExitStack()
            self.fmT = self.sb(r2, "fmT", [128, EC * 2048], BF16)
            self.phase_C(r2)
            r2.close()
        if not (STOP_AFTER.startswith("A") or STOP_AFTER in ("B", "C")):
            r3 = ExitStack()
            self.phase_D(r3)
            r3.close()
        if STOP_AFTER == "":
            r4 = ExitStack()
            self.phase_F(r4)
            r4.close()
        if STOP_AFTER:
            blk = self.block

            @blk.gpsimd
            def _(gg):
                for n in ("st", "ss", "sv"):
                    self.wait(gg, n, self.cnt[n])
        es.close()
        return nc

    def phase_A(self, r1):
        blk = self.block
        a1 = ExitStack()
        wkv_f = self.sb(a1, "wkv_f", [128, 2048], F32)
        wk_sb = self.sb(a1, "wk_sb", [128, G * EC * 128], BF16)
        wv_sb = self.sb(a1, "wv_sb", [128, G * EC * 128], BF16)
        xg = [self.sb(a1, f"xg{i}", [128, 2 * 2048], F32) for i in range(2)]
        xnb = [self.sb(a1, f"xnb{i}", [128, 2 * 2048], BF16) for i in range(2)]
        xnT = [self.sb(a1, f"xnT{i}", [128, EC * 256], BF16) for i in range(2)]
        self.ptp = [self.ps(a1, f"ptpA{i}", [128, 512], BF16) for i in range(2)]
        psA = [self.ps(a1, f"psA{i}", [128, 512]) for i in range(2)]
        psB = [self.ps(a1, f"psB{i}", [128, 512]) for i in range(2)]
        self._tp_guard = {}

        wkv_done = 0
        for g in range(2 * G):
            w_ap = (self.wk if g < G else self.wv)[g % G]
            dst = (wk_sb if g < G else wv_sb)

            @blk.sync
            def _(sy, w_ap=w_ap, wkv_done=wkv_done):
                self.wait(sy, "sv", wkv_done)
                i = sy.dma_start(out=wkv_f[:], in_=w_ap.rearrange("(a p) d -> p a d", p=128))
                self.inc(i, "dwkv", 16)
            dd = self.cnt["dwkv"]

            @blk.vector
            def _(v, dst=dst, g=g, dd=dd):
                self.wait(v, "dwkv", dd)
                i = v.tensor_copy(out=dst[:, (g % G) * 2048:((g % G) + 1) * 2048], in_=wkv_f[:])
                self.inc(i, "sv")
            wkv_done = self.cnt["sv"]

        @blk.vector
        def _(v):
            v.memset(self.V_sb[:].rearrange("p (c x) -> p c x", x=VST)[:, :, 129:VST], 0.0)
            i = v.memset(self.V_sb[:].rearrange("p (c x) -> p c x", x=VST)[:, :, 128:129], 1.0)
            self.inc(i, "sv")
        vready = self.cnt["sv"]
        if STOP_AFTER == "A0":
            a1.close()
            return

        xg_cons = [0, 0]
        xnb_cons = [0, 0]
        k_evac = [0, 0]
        v_evac = [0, 0]
        ln_guard = 0
        ln_map = {}
        xn_map = {}
        xnT_map = {}

        def emit_aload(gi):
            bi = gi % 2

            @blk.sync
            def _(sy, gi=gi, bi=bi, g=xg_cons[bi]):
                self.wait(sy, "ss", g)
                i = sy.dma_start(
                    out=xg[bi][:].rearrange("p (j e) -> p j e", e=2048),
                    in_=self.xseq[gi * 256:(gi + 1) * 256, :].rearrange("(j p) e -> p j e", p=128))
                self.inc(i, "dxa%d" % bi, 16)

        def emit_aln(gi):
            bi = gi % 2
            dd = self.cnt["dxa%d" % bi]

            @blk.vector
            def _(v, bi=bi, dd=dd):
                self.wait(v, "dxa%d" % bi, dd)
            ln_map[gi] = self.ln_stats(xg[bi], 2, ln_guard)

        def emit_axn(gi):
            nonlocal ln_guard
            bi = gi % 2

            @blk.scalar
            def _(s, bi=bi, ln_done=ln_map[gi], g=xnb_cons[bi]):
                self.wait(s, "sv", ln_done)
                self.wait(s, "st", g)
                for j in range(2):
                    i = s.activation(out=xnb[bi][:, j * 2048:(j + 1) * 2048],
                                     in_=xg[bi][:, j * 2048:(j + 1) * 2048], func=AF.Identity,
                                     scale=self.stat[:, j:j + 1], bias=self.stat[:, 48 + j:49 + j])
                    self.inc(i, "ss")
            xn_map[gi] = self.cnt["ss"]
            xg_cons[bi] = self.cnt["ss"]
            ln_guard = self.cnt["ss"]

        def emit_atrans(gi):
            bi = gi % 2

            @blk.tensor
            def _(t, xn_done=xn_map[gi]):
                self.wait(t, "ss", xn_done)
                self.wait(t, "dini", self.setup_dx)
            for ec2 in range(0, EC, 2):
                def wr(v, pp, bi=bi, ec2=ec2):
                    return v.tensor_copy(out=xnT[bi][:, ec2 * 256: ec2 * 256 + 512], in_=pp[:])
                cols = [(ec2 + q // 2) * 128 + (q % 2) * 2048 for q in range(4)]
                self.tp4list(xnb[bi], cols, wr)
            xnb_cons[bi] = self.cnt["st"]
            xnT_map[gi] = self.cnt["sv"]

        emit_aload(0)
        emit_aln(0)
        emit_axn(0)
        emit_atrans(0)
        for gi in range(16):
            bi = gi % 2
            if gi + 1 < 16:
                emit_aload(gi + 1)

            for g in range(G):
                pk = psA[g % 2]

                @blk.tensor
                def _(t, g=g, bi=bi, xnT_done=xnT_map[gi], pk=pk, ke=k_evac[g % 2],
                      vready=vready, wkv_done=wkv_done):
                    self.wait(t, "sv", max(xnT_done, vready, wkv_done))
                    self.wait(t, "ss", ke)
                    i = None
                    for ec in range(EC):
                        i = t.matmul(pk[:, 0:256],
                                     wk_sb[:, (g * EC + ec) * 128:(g * EC + ec) * 128 + 128],
                                     xnT[bi][:, ec * 256:(ec + 1) * 256],
                                     start=(ec == 0), stop=(ec == EC - 1))
                    self.inc(i, "st")
                stv = self.cnt["st"]

                @blk.scalar
                def _(s, g=g, gi=gi, pk=pk, stv=stv):
                    self.wait(s, "st", stv)
                    i = s.activation(out=self.K_sb[:, g * S + gi * 256: g * S + gi * 256 + 256],
                                     in_=pk[:, 0:256], func=AF.Identity, bias=self.bk_sb[:, g:g + 1])
                    self.inc(i, "ss")
                k_evac[g % 2] = self.cnt["ss"]

                pv = psB[g % 2]

                @blk.tensor
                def _(t, g=g, bi=bi, pv=pv, ve=v_evac[g % 2]):
                    self.wait(t, "sv", ve)
                    i = None
                    for j in range(2):
                        for ec in range(EC):
                            i = t.matmul(pv[:, j * 128:(j + 1) * 128],
                                         xnT[bi][:, ec * 256 + j * 128: ec * 256 + j * 128 + 128],
                                         wv_sb[:, (g * EC + ec) * 128:(g * EC + ec) * 128 + 128],
                                         start=(ec == 0), stop=(ec == EC - 1))
                    self.inc(i, "st")
                stv = self.cnt["st"]

                @blk.vector
                def _(v, g=g, gi=gi, pv=pv, stv=stv):
                    self.wait(v, "st", stv)
                    dst = self.V_sb[:].rearrange("p (c x) -> p c x", x=VST)
                    bvv = self.flat_ap(self.bvb_sb, g * 128, [[G * 128, 128], [0, 2], [1, 128]])
                    i = v.scalar_tensor_tensor(
                        out=dst[:, g * NTS + gi * 2: g * NTS + gi * 2 + 2, 0:128],
                        in0=pv[:, 0:256].rearrange("p (j d) -> p j d", d=128),
                        scalar=1.0, in1=bvv, op0=ALU.mult, op1=ALU.add)
                    self.inc(i, "sv")
                v_evac[g % 2] = self.cnt["sv"]

                if gi + 1 < 16:
                    if g == 1:
                        emit_aln(gi + 1)
                    elif g == 2:
                        emit_axn(gi + 1)
            if gi + 1 < 16:
                emit_atrans(gi + 1)

            if DEBUG_SERIAL:
                self.serial_barrier()

            if DEBUG_DUMP and gi == 15:
                @blk.gpsimd
                def _(gg, bi=bi, sv=self.cnt["sv"], ss=self.cnt["ss"]):
                    self.wait(gg, "sv", sv)
                    self.wait(gg, "ss", ss)
                    i = gg.dma_start(out=self.dbg_a[:, 0:4096], in_=xnb[bi][:])
                    self.inc(i, "dmk", 16)
                    i = gg.dma_start(out=self.dbg_a[:, 4096:4096 + 4096],
                                     in_=xnT[bi][:])
                    self.inc(i, "dmk", 16)
                    i = gg.dma_start(out=self.dbg_a[:, 2 * 4096:2 * 4096 + 4096], in_=xg[bi][:])
                    self.inc(i, "dmk", 16)
                    i = gg.dma_start(out=self.dbg_a[:, 3 * 4096:3 * 4096 + 64], in_=self.stat[:])
                    self.inc(i, "dmk", 16)
                    gg.wait_ge(self.sems["dmk"], self.cnt["dmk"])

        a1.close()
        if STOP_AFTER == "A1":
            return
        a1_end = dict(st=self.cnt["st"], ss=self.cnt["ss"], sv=self.cnt["sv"])

        @blk.sync
        def _(sy):
            self.wait(sy, "st", a1_end["st"])
            self.wait(sy, "ss", a1_end["ss"])
            self.wait(sy, "sv", a1_end["sv"])

        self.Q_sb = self.sb(r1, "Q_sb", [128, NSLOT * 2048], BF16)
        a2 = ExitStack()
        self.ptp = [self.ps(a2, f"ptpA2{i}", [128, 512], BF16) for i in range(2)]
        psA = [self.ps(a2, f"psA2{i}", [128, 512]) for i in range(2)]
        self._tp_guard = {}
        wq_b = [self.sb(a2, f"wq_b{i}", [128, 2048], BF16) for i in range(2)]
        xo = [self.sb(a2, f"xo{i}", [128, 2048], F32) for i in range(2)]
        xob = [self.sb(a2, f"xob{i}", [128, 2048], BF16) for i in range(2)]
        xnT2 = self.sb(a2, "xnT2", [128, EC * 512], BF16)

        wq_cons = [0, 0]
        wqb_cons = [0, 0]
        xo_cons = [0, 0]
        xob_cons = [0, 0]
        q_evac = [0, 0]
        a1_barrier = dict(ss=self.cnt["ss"], sv=self.cnt["sv"])
        for tg in range(4):
            for jj in range(4):
                tc = tg * 4 + jj
                bi = tc % 2

                @blk.sync
                def _(sy, tc=tc, bi=bi, g=xo_cons[bi]):
                    self.wait(sy, "ss", g)
                    i = sy.dma_start(out=xo[bi][:], in_=self.xown[tc * 128:(tc + 1) * 128, :])
                    self.inc(i, "dxa%d" % bi, 16)
                dd = self.cnt["dxa%d" % bi]

                @blk.vector
                def _(v, bi=bi, dd=dd):
                    self.wait(v, "dxa%d" % bi, dd)
                ln_done = self.ln_stats(xo[bi], 1, ln_guard)

                @blk.scalar
                def _(s, bi=bi, ln_done=ln_done, g=xob_cons[bi]):
                    self.wait(s, "sv", ln_done)
                    self.wait(s, "st", g)
                    i = s.activation(out=xob[bi][:], in_=xo[bi][:], func=AF.Identity,
                                     scale=self.stat[:, 0:1], bias=self.stat[:, 48:49])
                    self.inc(i, "ss")
                xn_done = self.cnt["ss"]
                xo_cons[bi] = xn_done
                ln_guard = xn_done

                @blk.tensor
                def _(t, xn_done=xn_done):
                    self.wait(t, "ss", xn_done)
                for ec4 in range(0, EC, 4):
                    def wr(v, pp, ec4=ec4, jj=jj):
                        dst = xnT2[:].rearrange("p (a q) -> p a q", q=512)
                        return v.tensor_copy(
                            out=dst[:, ec4:ec4 + 4, jj * 128:(jj + 1) * 128],
                            in_=pp[:].rearrange("p (a q) -> p a q", q=128))
                    self.tp4(xob[bi], ec4 * 128, 4, wr)
                xob_cons[bi] = self.cnt["st"]
            xnT2_done = self.cnt["sv"]

            for h in range(H):
                bi = h % 2

                @blk.gpsimd
                def _(gg, tg=tg, h=h, bi=bi, g=wqb_cons[bi]):
                    if tg == 0 and h == 0:
                        self.wait(gg, "st", a1_end["st"])
                        self.wait(gg, "ss", a1_end["ss"])
                        self.wait(gg, "sv", a1_end["sv"])
                    self.wait(gg, "st", g)
                    i = gg.dma_start(out=wq_b[bi][:],
                                     in_=self.wq[h].rearrange("(a p) d -> p a d", p=128))
                    self.inc(i, "dwq%d" % bi, 16)
                cast_done = self.cnt["dwq%d" % bi]

                pq = psA[h % 2]

                @blk.tensor
                def _(t, h=h, tg=tg, bi=bi, pq=pq, cast_done=cast_done,
                      xnT2_done=xnT2_done, qe=q_evac[h % 2], ab=a1_barrier):
                    self.wait(t, "dwq%d" % bi, cast_done)
                    self.wait(t, "sv", xnT2_done)
                    self.wait(t, "ss", max(qe, ab["ss"]))
                    i = None
                    for ec in range(EC):
                        i = t.matmul(pq[:, 0:512], wq_b[bi][:, ec * 128:(ec + 1) * 128],
                                     xnT2[:, ec * 512:(ec + 1) * 512],
                                     start=(ec == 0), stop=(ec == EC - 1))
                    self.inc(i, "st")
                stv = self.cnt["st"]
                wqb_cons[bi] = stv

                @blk.scalar
                def _(s, h=h, tg=tg, pq=pq, stv=stv):
                    self.wait(s, "st", stv)
                    g2, hh = h // 4, h % 4
                    dst = self.Q_sb[:].rearrange("p (t x) -> p t x", x=2048)
                    i = s.activation(
                        out=dst[:, tg * 4:(tg + 1) * 4,
                                g2 * 512 + hh * 128: g2 * 512 + hh * 128 + 128],
                        in_=pq[:].rearrange("p (a q) -> p a q", q=128),
                        func=AF.Identity, bias=self.bq_sb[:, h:h + 1])
                    self.inc(i, "ss")
                q_evac[h % 2] = self.cnt["ss"]
        if DEBUG_DUMP:
            @blk.gpsimd
            def _(gg):
                self.wait(gg, "ss", self.cnt["ss"])
                self.wait(gg, "sv", self.cnt["sv"])
                i = gg.dma_start(out=self.dbg_q, in_=self.Q_sb[:])
                self.inc(i, "dmk", 16)
                i = gg.dma_start(out=self.dbg_k, in_=self.K_sb[:])
                self.inc(i, "dmk", 16)
                i = gg.dma_start(out=self.dbg_v, in_=self.V_sb[:])
                self.inc(i, "dmk", 16)
                gg.wait_ge(self.sems["dmk"], self.cnt["dmk"])
        a2.close()
        self.A_end = dict(ss=self.cnt["ss"], sv=self.cnt["sv"], st=self.cnt["st"])

    def phase_B(self, r1):
        blk = self.block
        b1 = ExitStack()
        mk_sb = self.sb(b1, "mk_sb", [128, NSLOT * 2 * 128], F32)
        pT = [self.sb(b1, f"pT{i}", [128, 1024], BF16) for i in range(2)]
        attn_sb = [self.sb(b1, f"attn_sb{i}", [128, 2048], F32) for i in range(2)]
        linv = self.sb(b1, "linv", [128, 8], F32)
        stp = [self.ps(b1, f"stp{i}", [128, 1024]) for i in range(2)]
        acc = [self.ps(b1, f"acc{i}", [128, 512]) for i in range(4)]

        @blk.sync
        def _(sy):
            self.wait(sy, "st", self.A_end["st"])
            self.wait(sy, "ss", self.A_end["ss"])
            self.wait(sy, "sv", self.A_end["sv"])
            i = sy.dma_start(out=mk_sb[:], in_=self.mk)
            self.inc(i, "dmk", 16)
        mk_done = self.cnt["dmk"]

        @blk.scalar
        def _(s):
            s.preload_act_table(AF.Exp)

        stp_exp = [0, 0]
        pT_av = [0, 0]
        attn_dma = [0, 0]
        acc_free = 0
        first = True
        for s_ in range(NSLOT):
            for gp in range(2):
                av_cnt = {}
                exp_cnt = {}
                for j in range(kstar(s_)):
                    pbuf = j % 2
                    KS = kstar(s_)

                    @blk.tensor
                    def _(t, s_=s_, gp=gp, j=j, pbuf=pbuf, first=first,
                          se=stp_exp[pbuf]):
                        if first:
                            self.wait(t, "ss", self.A_end["ss"])
                            self.wait(t, "sv", self.A_end["sv"])
                        self.wait(t, "ss", se)
                        i = None
                        for g2 in range(2):
                            g = gp * 2 + g2
                            i = t.matmul(stp[pbuf][:, g2 * 512:(g2 + 1) * 512],
                                         self.K_sb[:, g * S + j * 128: g * S + j * 128 + 128],
                                         self.Q_sb[:, s_ * 2048 + g * 512: s_ * 2048 + g * 512 + 512],
                                         start=True, stop=True)
                        self.inc(i, "st")
                    first = False
                    sc_done = self.cnt["st"]

                    masked = j >= KS - 2
                    msk_done = 0
                    if masked:
                        @blk.vector
                        def _(v, s_=s_, j=j, pbuf=pbuf, sc_done=sc_done, KS=KS):
                            self.wait(v, "st", sc_done)
                            self.wait(v, "dmk", mk_done)
                            mi = (s_ * 2 + (j - (KS - 2))) * 128
                            src = self.flat_ap(mk_sb, mi,
                                               [[NSLOT * 2 * 128, 128], [0, 8], [1, 128]])
                            i = v.tensor_tensor(
                                out=stp[pbuf][:].rearrange("p (a q) -> p a q", q=128),
                                in0=stp[pbuf][:].rearrange("p (a q) -> p a q", q=128),
                                in1=src, op=ALU.add)
                            self.inc(i, "sv")
                        msk_done = self.cnt["sv"]

                    @blk.scalar
                    def _(s, j=j, pbuf=pbuf, sc_done=sc_done, masked=masked,
                          msk_done=msk_done, pa=pT_av[pbuf]):
                        if masked:
                            self.wait(s, "sv", msk_done)
                        else:
                            self.wait(s, "st", sc_done)
                        self.wait(s, "st", pa)
                        i = s.activation(out=pT[pbuf][:], in_=stp[pbuf][:], func=AF.Exp,
                                         scale=SCL, bias=self.consts[:, 1:2])
                        self.inc(i, "ss")
                    exp_cnt[j] = self.cnt["ss"]
                    stp_exp[pbuf] = exp_cnt[j]

                    @blk.tensor
                    def _(t, s_=s_, gp=gp, j=j, pbuf=pbuf, KS=KS, ec=exp_cnt[j], af=acc_free):
                        self.wait(t, "ss", ec)
                        if j == 0:
                            self.wait(t, "sv", af)
                        vv = self.V_sb[:].rearrange("p (c x) -> p c x", x=VST)
                        i = None
                        for g2 in range(2):
                            g = gp * 2 + g2
                            for hh in range(4):
                                h8 = g2 * 4 + hh
                                i = t.matmul(acc[h8 // 2][:, (h8 % 2) * 136:(h8 % 2) * 136 + 129],
                                             pT[pbuf][:, h8 * 128:(h8 + 1) * 128],
                                             vv[:, g * NTS + j, 0:129],
                                             start=(j == 0 and h8 % 2 == 0),
                                             stop=(j == KS - 1 and h8 % 2 == 1))
                        self.inc(i, "st")
                    av_cnt[j] = self.cnt["st"]
                    pT_av[pbuf] = av_cnt[j]

                if DEBUG_DUMP and s_ == 15 and gp == 1:
                    @blk.vector
                    def _(v, av=av_cnt[kstar(s_) - 1]):
                        self.wait(v, "st", av)
                        v.tensor_copy(out=self.junk_s[:, 0:1024], in_=stp[0][:])
                        i = v.tensor_copy(out=self.junk_s2[:, 0:1024], in_=stp[1][:])
                        self.inc(i, "sv")

                    @blk.vector
                    def _(v, av=av_cnt[kstar(s_) - 1]):
                        v.tensor_copy(out=self.junk_s[:, 1024:2048], in_=self.K_sb[:, 12 * S // 4:12 * S // 4 + 1024])
                        i = v.tensor_copy(out=self.junk_s2[:, 1024:2048], in_=self.Q_sb[:, 15 * 2048:15 * 2048 + 1024])
                        self.inc(i, "sv")

                    @blk.gpsimd
                    def _(gg, sv=self.cnt["sv"], ec=exp_cnt[kstar(s_) - 1]):
                        self.wait(gg, "sv", sv)
                        self.wait(gg, "ss", ec)
                        i = gg.dma_start(out=self.dbg_c[:, 0:1024], in_=self.junk_s[:, 1024:2048])
                        self.inc(i, "dmk", 16)
                        i = gg.dma_start(out=self.dbg_c[:, 1024:2048], in_=self.junk_s2[:, 1024:2048])
                        self.inc(i, "dmk", 16)
                        i = gg.dma_start(out=self.dbg_b[:, 0:1024], in_=self.junk_s[:, 0:1024])
                        self.inc(i, "dmk", 16)
                        i = gg.dma_start(out=self.dbg_b[:, 1024:2048], in_=self.junk_s2[:, 0:1024])
                        self.inc(i, "dmk", 16)
                        i = gg.dma_start(out=self.dbg_b[:, 2048:3072], in_=pT[0][:])
                        self.inc(i, "dmk", 16)
                        i = gg.dma_start(out=self.dbg_b[:, 3072:4096], in_=pT[1][:])
                        self.inc(i, "dmk", 16)

                @blk.vector
                def _(v, s_=s_, gp=gp, av=av_cnt[kstar(s_) - 1], ad=attn_dma[s_ % 2]):
                    if DEBUG_DUMP and s_ == 15 and gp == 1:
                        self.wait(v, "dmk", self.cnt["dmk"])
                    self.wait(v, "st", av)
                    self.wait(v, "datt%d" % (s_ % 2), ad)
                    i = None
                    for h8 in range(8):
                        i = v.reciprocal(out=linv[:, h8:h8 + 1],
                                         in_=acc[h8 // 2][:, (h8 % 2) * 136 + 128:(h8 % 2) * 136 + 129])
                    self.fence(i, v, "sv")
                    for h8 in range(8):
                        g2, hh = h8 // 4, h8 % 4
                        head = (gp * 2 + g2) * 4 + hh
                        i = v.tensor_scalar_mul(
                            out=attn_sb[s_ % 2][:, head * 128:(head + 1) * 128],
                            in0=acc[h8 // 2][:, (h8 % 2) * 136:(h8 % 2) * 136 + 128],
                            scalar1=linv[:, h8:h8 + 1])
                    self.inc(i, "sv")
                acc_free = self.cnt["sv"]
            evac_done = self.cnt["sv"]

            @blk.gpsimd
            def _(gg, s_=s_, evac_done=evac_done):
                self.wait(gg, "sv", evac_done)
                i = gg.dma_start(out=self.attn_s[s_ * 128:(s_ + 1) * 128, :],
                                 in_=attn_sb[s_ % 2][:])
                self.inc(i, "datt%d" % (s_ % 2), 16)
            attn_dma[s_ % 2] = self.cnt["datt%d" % (s_ % 2)]
            if DEBUG_SERIAL_B:
                self.serial_barrier()
        b1.close()
        self.B_end = dict(st=self.cnt["st"], datt0=self.cnt["datt0"],
                          datt1=self.cnt["datt1"])

    def phase_C(self, r2):
        blk = self.block
        c1 = ExitStack()
        xo = [self.sb(c1, f"cxo{i}", [128, 2048], F32) for i in range(2)]
        xmm = [self.sb(c1, f"xmm{i}", [128, 2048], F32) for i in range(2)]
        qx8 = self.sb(c1, "qx8", [128, 2048], FP8)
        qxd = [self.sb(c1, f"qxd{i}", [128, 2048], BF16) for i in range(2)]
        self.ptp = [self.ps(c1, f"ptpC{i}", [128, 512], BF16) for i in range(2)]
        self._tp_guard = {}

        @blk.scalar
        def _(s):
            self.wait(s, "st", self.B_end["st"])
            s.preload_act_table(AF.Sqrt)

        xo_cons = [0, 0]
        lng = [0, 0]
        bankg = [0, 0]
        qxd_tr = [0, 0]
        ln_map = {}
        sc_map = {}
        qd_map = {}

        def emit_c0_load(tc):
            bi = tc % 2

            @blk.sync
            def _(sy, tc=tc, bi=bi, g=xo_cons[bi]):
                if tc == 0:
                    self.wait(sy, "st", self.B_end["st"])
                    self.wait(sy, "datt0", self.B_end["datt0"])
                    self.wait(sy, "datt1", self.B_end["datt1"])
                self.wait(sy, "ss", g)
                i = sy.dma_start(out=xo[bi][:], in_=self.xown[tc * 128:(tc + 1) * 128, :])
                self.inc(i, "dxa%d" % bi, 16)

        def emit_c0_ln(tc):
            bi = tc % 2
            dd = self.cnt["dxa%d" % bi]

            @blk.vector
            def _(v, bi=bi, dd=dd):
                self.wait(v, "dxa%d" % bi, dd)
            ln_map[tc] = self.ln_stats(xo[bi], 1, lng[bi], o=2 + bi, jb=bi)

        def emit_c0_q(tc):
            bi = tc % 2
            base = 4 if bi == 0 else 12

            @blk.scalar
            def _(s, bi=bi, ln_done=ln_map[tc]):
                self.wait(s, "sv", ln_done)
                i = s.activation(out=xmm[bi][:], in_=xo[bi][:], func=AF.Identity,
                                 scale=self.stat[:, 2 + bi:3 + bi],
                                 bias=self.stat[:, 50 + bi:51 + bi])
                self.inc(i, "ss")
            xmm_done = self.cnt["ss"]
            xo_cons[bi] = xmm_done
            lng[bi] = xmm_done

            sc_map[tc] = self.qscales(xmm[bi][:], max(xmm_done, bankg[bi]), base=base)

            @blk.scalar
            def _(s, bi=bi, base=base, sc_done=sc_map[tc], qt=qxd_tr[bi]):
                self.wait(s, "sv", sc_done)
                self.wait(s, "st", qt)
                i = s.activation(out=qx8[:], in_=xmm[bi][:], func=AF.Copy,
                                 scale=self.stat[:, base + 2:base + 3])
                self.fence(i, s, "ss")
                i = s.activation(out=qxd[bi][:], in_=qx8[:], func=AF.Copy,
                                 scale=self.stat[:, base + 3:base + 4])
                self.inc(i, "ss")
            qd_map[tc] = self.cnt["ss"]
            bankg[bi] = self.cnt["ss"]

        def emit_c0_trans(tc):
            bi = tc % 2

            @blk.tensor
            def _(t, qxd_done=qd_map[tc]):
                self.wait(t, "ss", qxd_done)
            for ec4 in range(0, EC, 4):
                def wr(v, pp, ec4=ec4, tc=tc):
                    dst = self.fmT[:].rearrange("p (a q) -> p a q", q=2048)
                    return v.tensor_copy(
                        out=dst[:, ec4:ec4 + 4, tc * 128:(tc + 1) * 128],
                        in_=pp[:].rearrange("p (a q) -> p a q", q=128))
                self.tp4(qxd[bi], ec4 * 128, 4, wr)
            qxd_tr[bi] = self.cnt["st"]

        emit_c0_load(0)
        emit_c0_ln(0)
        emit_c0_q(0)
        for tc in range(NSLOT):
            if tc + 1 < NSLOT:
                emit_c0_load(tc + 1)
                emit_c0_ln(tc + 1)
            emit_c0_trans(tc)
            if tc + 1 < NSLOT:
                emit_c0_q(tc + 1)
        stat_guard = max(bankg)
        qxT_done = self.cnt["sv"]

        @blk.vector
        def _(v):
            i = v.memset(self.tmax[:], 0.0)
            self.inc(i, "sv")
        tmax_init = self.cnt["sv"]

        wgf = self.sb(c1, "wgf", [128, 2 * 2048], F32)
        wuf = self.sb(c1, "wuf", [128, 2 * 2048], F32)
        w8 = self.sb(c1, "w8", [128, 2048], FP8)
        qwT = [self.sb(c1, f"qwT{i}", [128, EC * 256], BF16) for i in range(2)]
        quT = [self.sb(c1, f"quT{i}", [128, EC * 256], BF16) for i in range(2)]
        wdq = self.sb(c1, "wdq", [128, 2048], BF16)
        sg_sb = [self.sb(c1, f"sg_sb{i}", [128, 512], F32) for i in range(2)]
        tsl = self.sb(c1, "tsl", [128, 8 * 256], F32)
        psG = [self.ps(c1, f"psG{i}", [128, 512]) for i in range(2)]
        psU = [self.ps(c1, f"psU{i}", [128, 512]) for i in range(2)]

        @blk.scalar
        def _(s):
            s.preload_act_table(AF.Silu)

        NHS = 2 * NSL
        w_cons = [0, 0]
        tsl_dma_last = 0
        gu_evac = [0, 0]
        wdq_trans = 0
        mm_done_hist = {}
        wdma_cnt = {}
        qd_cnt = {}
        bank_guard = [stat_guard, stat_guard]
        prep_sv = {}

        def emit_wload(p, wi):
            wf, wsrc = (wgf, self.wg) if wi == 0 else (wuf, self.wu)
            wsem = "dwg" if wi == 0 else "dwu"

            @blk.sync
            def _(sy, p=p, wf=wf, wsrc=wsrc, wsem=wsem, wc=w_cons[wi]):
                self.wait(sy, "ss", wc)
                r0 = p * 256
                i = sy.dma_start(
                    out=wf[:].rearrange("p (c e) -> p c e", e=2048),
                    in_=wsrc[r0:r0 + 256, :].rearrange("(c p) e -> p c e", p=128))
                self.inc(i, wsem, 16)
            wdma_cnt[(p, wi)] = self.cnt[wsem]

        def emit_wquant(p, g):
            wi, cc = g // 2, g % 2
            wf = wgf if wi == 0 else wuf
            wsem = "dwg" if wi == 0 else "dwu"
            base = 17 if g % 2 == 0 else 24
            sc_done = self.qscales(wf[:, cc * 2048:(cc + 1) * 2048],
                                   bank_guard[g % 2], wsem, wdma_cnt[(p, wi)],
                                   base=base)

            @blk.scalar
            def _(s, wf=wf, cc=cc, sc_done=sc_done, wt=wdq_trans, base=base):
                self.wait(s, "sv", sc_done)
                self.wait(s, "st", wt)
                i = s.activation(out=w8[:], in_=wf[:, cc * 2048:(cc + 1) * 2048],
                                 func=AF.Copy, scale=self.stat[:, base + 2:base + 3])
                self.fence(i, s, "ss")
                i = s.activation(out=wdq[:], in_=w8[:], func=AF.Copy,
                                 scale=self.stat[:, base + 3:base + 4])
                self.inc(i, "ss")
            qd_cnt[(p, g)] = self.cnt["ss"]
            bank_guard[g % 2] = self.cnt["ss"]
            if cc == 1:
                w_cons[wi] = self.cnt["ss"]

        def emit_wtrans(p, g):
            nonlocal wdq_trans
            wi, cc = g // 2, g % 2
            wT2 = qwT if wi == 0 else quT
            buf = p % 2

            @blk.tensor
            def _(t, qd=qd_cnt[(p, g)]):
                self.wait(t, "ss", qd)
            mmg = mm_done_hist.get(p - 2, 0)
            for ec4 in range(0, EC, 4):
                def wr(v, pp, wT2=wT2, buf=buf, cc=cc, ec4=ec4, mmg=mmg):
                    self.wait(v, "st", mmg)
                    dst = wT2[buf][:].rearrange("p (a q) -> p a q", q=256)
                    return v.tensor_copy(
                        out=dst[:, ec4:ec4 + 4, cc * 128:(cc + 1) * 128],
                        in_=pp[:].rearrange("p (a q) -> p a q", q=128))
                self.tp4(wdq, ec4 * 128, 4, wr)
            wdq_trans = self.cnt["st"]
            if g == 3:
                prep_sv[p] = self.cnt["sv"]

        emit_wload(0, 0)
        emit_wload(0, 1)
        for g in range(4):
            emit_wquant(0, g)
            emit_wtrans(0, g)
        emit_wload(1, 0)
        emit_wload(1, 1)
        for hs in range(NHS):
            buf = hs % 2
            p = hs + 1
            if p < NHS:
                emit_wquant(p, 0)
            wq_done = prep_sv[hs]

            for tc in range(NSLOT):
                pg, pu = psG[tc % 2], psU[tc % 2]
                slot = tc % 8

                @blk.tensor
                def _(t, hs=hs, tc=tc, buf=buf, pg=pg, pu=pu, wq_done=wq_done,
                      ge=gu_evac[tc % 2], qxT_done=qxT_done):
                    if tc == 0:
                        self.wait(t, "sv", max(wq_done, qxT_done))
                    self.wait(t, "sv", ge)
                    for ec in range(EC):
                        t.matmul(pg[:, 0:256],
                                 self.fmT[:, ec * 2048 + tc * 128: ec * 2048 + tc * 128 + 128],
                                 qwT[buf][:, ec * 256:(ec + 1) * 256],
                                 start=(ec == 0), stop=(ec == EC - 1))
                    i = None
                    for ec in range(EC):
                        i = t.matmul(pu[:, 0:256],
                                     self.fmT[:, ec * 2048 + tc * 128: ec * 2048 + tc * 128 + 128],
                                     quT[buf][:, ec * 256:(ec + 1) * 256],
                                     start=(ec == 0), stop=(ec == EC - 1))
                    self.inc(i, "st")
                gu_done = self.cnt["st"]

                if p < NHS and tc in (2, 5, 8, 11):
                    g = (2, 5, 8, 11).index(tc)
                    emit_wtrans(p, g)
                    if g < 3:
                        emit_wquant(p, g + 1)
                    elif p + 1 < NHS:
                        emit_wload(p + 1, 1)
                if p + 1 < NHS and tc == 5:
                    emit_wload(p + 1, 0)

                @blk.scalar
                def _(s, tc=tc, pg=pg, gu_done=gu_done, ge=gu_evac[tc % 2]):
                    self.wait(s, "st", gu_done)
                    self.wait(s, "sv", ge)
                    i = s.activation(out=sg_sb[tc % 2][:, 0:256], in_=pg[:, 0:256], func=AF.Silu)
                    self.inc(i, "ss")
                sg_done = self.cnt["ss"]

                @blk.vector
                def _(v, hs=hs, tc=tc, pu=pu, slot=slot, sg_done=sg_done,
                      td=tsl_dma_last, tmax_init=tmax_init):
                    self.wait(v, "ss", sg_done)
                    self.wait(v, "sv", tmax_init)
                    self.wait(v, "dtsl", td)
                    i = v.tensor_mul(out=tsl[:, slot * 256:(slot + 1) * 256],
                                     in0=sg_sb[tc % 2][:, 0:256], in1=pu[:, 0:256])
                    self.fence(i, v, "sv")
                    i = v.reduce_max(out=self.stat[:, 22:23], in_=tsl[:, slot * 256:(slot + 1) * 256],
                                     axis=AX.X, apply_absolute_value=True)
                    self.fence(i, v, "sv")
                    i = v.tensor_max(out=self.tmax[:, tc:tc + 1], in0=self.tmax[:, tc:tc + 1],
                                     in1=self.stat[:, 22:23])
                    self.inc(i, "sv")
                gu_evac[tc % 2] = self.cnt["sv"]

                if tc % 8 == 7:
                    tch = tc // 8
                    t_done = self.cnt["sv"]

                    @blk.gpsimd
                    def _(gg, hs=hs, tch=tch, t_done=t_done):
                        self.wait(gg, "sv", t_done)
                        dst = self.t_s[:, hs * 256:(hs + 1) * 256]\
                            .rearrange("(t p) i -> p t i", p=128)
                        i = gg.dma_start(out=dst[:, tch * 8:(tch + 1) * 8, :],
                                         in_=tsl[:].rearrange("p (t i) -> p t i", i=256))
                        self.inc(i, "dtsl", 16)
                    tsl_dma_last = self.cnt["dtsl"]
            mm_done_hist[hs] = self.cnt["st"]
        stat_guard = max(bank_guard)
        if DEBUG_DUMP:
            @blk.gpsimd
            def _(gg):
                for n in ("st", "ss", "sv"):
                    self.wait(gg, n, self.cnt[n])
                i = gg.dma_start(out=self.dbg_d[:, 0:EC * 2048], in_=self.fmT[:])
                self.inc(i, "dmk", 16)
                i = gg.dma_start(out=self.dbg_d[:, EC * 2048:EC * 2048 + EC * 256], in_=qwT[0][:])
                self.inc(i, "dmk", 16)
                i = gg.dma_start(out=self.dbg_d[:, EC * 2048 + EC * 512:EC * 2048 + EC * 512 + EC * 256],
                                 in_=quT[0][:])
                self.inc(i, "dmk", 16)
                i = gg.dma_start(out=self.dbg_d[:, EC * 2048 + 2 * EC * 512:EC * 2048 + 2 * EC * 512 + 64],
                                 in_=self.stat[:])
                self.inc(i, "dmk", 16)
                i = gg.dma_start(out=self.dbg_d[:, EC * 2048 + 2 * EC * 512 + 64:],
                                 in_=self.tmax[:])
                self.inc(i, "dmk", 16)
                gg.wait_ge(self.sems["dmk"], self.cnt["dmk"])
        c1.close()
        self.C_end = dict(st=self.cnt["st"], dtsl=self.cnt["dtsl"], ss=self.cnt["ss"])
        self._c_stat_guard = stat_guard

    def phase_D(self, r3):
        blk = self.block
        d1 = ExitStack()
        qwdT = self.sb(d1, "qwdT", [128, ICH * 1024], BF16)
        wdf = self.sb(d1, "wdf", [128, II], F32)
        w8d = self.sb(d1, "w8d", [128, II], FP8)
        wddq = self.sb(d1, "wddq", [128, II], BF16)
        tt = self.sb(d1, "tt", [128, II], F32)
        qt8 = w8d
        qtd = self.sb(d1, "qtd", [128, II], BF16)
        qtT = [self.sb(d1, f"qtT{i}", [128, ICH * 128], BF16) for i in range(2)]
        ysb = self.sb(d1, "ysb", [128, 1024], F32)
        self.ptp = [self.ps(d1, f"ptpD{i}", [128, 512], BF16) for i in range(2)]
        self._tp_guard = {}
        psY = [self.ps(d1, f"psY{i}", [128, 512]) for i in range(2)]

        stat_guard = self._c_stat_guard
        mm_hist = {}
        wdf_cons = 0
        wddq_trans = 0
        tt_cons = 0
        qtd_trans = 0
        y_evac = 0
        y_dma = [0, 0]
        wdma_map = {}
        wqd_map = {}
        qwd_done = {}

        def emit_wdload(oh, occ):
            @blk.sync
            def _(sy, oh=oh, occ=occ, wc=wdf_cons):
                if oh == 0 and occ == 0:
                    self.wait(sy, "st", self.C_end["st"])
                    self.wait(sy, "dtsl", self.C_end["dtsl"])
                self.wait(sy, "ss", wc)
                r0 = oh * 1024 + occ * 128
                i = sy.dma_start(out=wdf[:], in_=self.wd[r0:r0 + 128, :])
                self.inc(i, "dwd", 16)
            wdma_map[(oh, occ)] = self.cnt["dwd"]

        def emit_wdquant(oh, occ):
            nonlocal stat_guard, wdf_cons
            sc_done = self.qscales(wdf[:], stat_guard, "dwd", wdma_map[(oh, occ)])

            @blk.scalar
            def _(s, sc_done=sc_done, wt=wddq_trans):
                self.wait(s, "sv", sc_done)
                self.wait(s, "st", wt)
                i = s.activation(out=w8d[:], in_=wdf[:], func=AF.Copy,
                                 scale=self.stat[:, 19:20])
                self.fence(i, s, "ss")
                i = s.activation(out=wddq[:], in_=w8d[:], func=AF.Copy,
                                 scale=self.stat[:, 20:21])
                self.inc(i, "ss")
            wdf_cons = self.cnt["ss"]
            stat_guard = self.cnt["ss"]
            wqd_map[(oh, occ)] = self.cnt["ss"]

        def emit_wdtrans(oh, occ):
            nonlocal wddq_trans

            @blk.tensor
            def _(t, qd=wqd_map[(oh, occ)]):
                self.wait(t, "ss", qd)
            for ic4 in range(0, ICH, 4):
                nn = min(4, ICH - ic4)

                def wr(v, pp, occ=occ, ic4=ic4, nn=nn):
                    dst = qwdT[:].rearrange("p (a q) -> p a q", q=1024)
                    return v.tensor_copy(
                        out=dst[:, ic4:ic4 + nn, occ * 128:(occ + 1) * 128],
                        in_=pp[:, 0:nn * 128].rearrange("p (a q) -> p a q", q=128))
                self.tp4(wddq, ic4 * 128, nn, wr)
            wddq_trans = self.cnt["st"]
            if occ == 7:
                qwd_done[oh] = self.cnt["sv"]

        def emit_dtc_loop(oh):
            nonlocal tt_cons, qtd_trans, y_evac
            tprep_sv = {}
            qtd_done_map = {}

            def emit_tload_quant(tc):
                nonlocal tt_cons

                @blk.gpsimd
                def _(gg, oh=oh, tc=tc, g=tt_cons):
                    self.wait(gg, "ss", g)
                    i = gg.dma_start(out=tt[:], in_=self.t_s[tc * 128:(tc + 1) * 128, :])
                    self.inc(i, "dtt", 16)
                tdma = self.cnt["dtt"]

                @blk.vector
                def _(v, tc=tc, tdma=tdma, sg=self._stat_guard):
                    self.wait(v, "dtt", tdma)
                    self.wait(v, "ss", sg)
                    i = v.reciprocal(out=self.stat[:, 25:26], in_=self.tmax[:, tc:tc + 1])
                    self.fence(i, v, "sv")
                    v.tensor_scalar_mul(out=self.stat[:, 26:27], in0=self.stat[:, 25:26],
                                        scalar1=224.0)
                    i = v.tensor_scalar_mul(out=self.stat[:, 27:28], in0=self.tmax[:, tc:tc + 1],
                                            scalar1=2.0 / 448.0)
                    self.inc(i, "sv")
                tsc_done = self.cnt["sv"]

                @blk.scalar
                def _(s, tsc_done=tsc_done, qt=qtd_trans):
                    self.wait(s, "sv", tsc_done)
                    self.wait(s, "st", qt)
                    i = s.activation(out=qt8[:], in_=tt[:], func=AF.Copy,
                                     scale=self.stat[:, 26:27])
                    self.fence(i, s, "ss")
                    i = s.activation(out=qtd[:], in_=qt8[:], func=AF.Copy,
                                     scale=self.stat[:, 27:28])
                    self.inc(i, "ss")
                tt_cons = self.cnt["ss"]
                self._stat_guard = self.cnt["ss"]
                qtd_done_map[tc] = self.cnt["ss"]

            def emit_ttrans(tc):
                nonlocal qtd_trans
                bi = tc % 2

                @blk.tensor
                def _(t, qtd_done=qtd_done_map[tc]):
                    self.wait(t, "ss", qtd_done)
                mmg = mm_hist.get(tc - 2 + oh * NSLOT, 0)
                for ic4 in range(0, ICH, 4):
                    nn = min(4, ICH - ic4)

                    def wr(v, pp, bi=bi, ic4=ic4, nn=nn, mmg=mmg):
                        self.wait(v, "st", mmg)
                        return v.tensor_copy(out=qtT[bi][:, ic4 * 128:(ic4 + nn) * 128],
                                             in_=pp[:, 0:nn * 128])
                    self.tp4(qtd, ic4 * 128, nn, wr)
                qtd_trans = self.cnt["st"]
                tprep_sv[tc] = self.cnt["sv"]

            emit_tload_quant(0)
            emit_ttrans(0)
            for tc in range(NSLOT):
                bi = tc % 2
                if tc + 1 < NSLOT:
                    emit_tload_quant(tc + 1)
                qtT_done = tprep_sv[tc]

                @blk.tensor
                def _(t, bi=bi, qtT_done=qtT_done, qd=qwd_done[oh], ye=y_evac):
                    self.wait(t, "sv", max(qtT_done, qd))
                    self.wait(t, "sv", ye)
                    for ic in range(ICH):
                        t.matmul(psY[0][:], qtT[bi][:, ic * 128:(ic + 1) * 128],
                                 qwdT[:, ic * 1024: ic * 1024 + 512],
                                 start=(ic == 0), stop=(ic == ICH - 1))
                    i = None
                    for ic in range(ICH):
                        i = t.matmul(psY[1][:], qtT[bi][:, ic * 128:(ic + 1) * 128],
                                     qwdT[:, ic * 1024 + 512: ic * 1024 + 1024],
                                     start=(ic == 0), stop=(ic == ICH - 1))
                    self.inc(i, "st")
                mm_done = self.cnt["st"]
                mm_hist[tc + oh * NSLOT] = mm_done

                if tc + 1 < NSLOT:
                    emit_ttrans(tc + 1)

                @blk.vector
                def _(v, mm_done=mm_done, yd=y_dma[0]):
                    self.wait(v, "st", mm_done)
                    self.wait(v, "dy0", yd)
                    v.tensor_copy(out=ysb[:, 0:512], in_=psY[0][:])
                    i = v.tensor_copy(out=ysb[:, 512:1024], in_=psY[1][:])
                    self.inc(i, "sv")
                y_evac = self.cnt["sv"]
                yv = self.cnt["sv"]

                @blk.sync
                def _(sy, oh=oh, tc=tc, yv=yv):
                    self.wait(sy, "sv", yv)
                    i = sy.dma_start(
                        out=self.y_s[tc * 128:(tc + 1) * 128, oh * 1024:(oh + 1) * 1024],
                        in_=ysb[:])
                    self.inc(i, "dy0", 16)
                y_dma[0] = self.cnt["dy0"]

        for oh in range(2):
            emit_wdload(oh, 0)
            for occ in range(8):
                emit_wdquant(oh, occ)
                if occ + 1 < 8:
                    emit_wdload(oh, occ + 1)
                emit_wdtrans(oh, occ)
            self._stat_guard = stat_guard
            emit_dtc_loop(oh)
        d1.close()
        self.D_end = dict(dy0=self.cnt["dy0"], dy1=self.cnt["dy1"],
                          sv=self.cnt["sv"], ss=self.cnt["ss"], st=self.cnt["st"],
                          dwd=self.cnt["dwd"], dtt=self.cnt["dtt"])

    def phase_F(self, r4):
        blk = self.block
        f1 = ExitStack()
        fx = [self.sb(f1, f"fx{i}", [128, 2048], F32) for i in range(2)]
        fa = [self.sb(f1, f"fa{i}", [128, 2048], F32) for i in range(2)]
        fy = [self.sb(f1, f"fy{i}", [128, 2048], F32) for i in range(2)]
        fo = [self.sb(f1, f"fo{i}", [128, 2048], F32) for i in range(2)]
        f_cons = [0, 0]
        out_dma = [0, 0]
        for tc in range(NSLOT):
            bi = tc % 2

            @blk.sync
            def _(sy, tc=tc, bi=bi, g=f_cons[bi]):
                if tc == 0:
                    for n in ("sv", "ss", "st", "dtt"):
                        self.wait(sy, n, self.D_end[n])
                self.wait(sy, "sv", g)
                i = sy.dma_start(out=fx[bi][:], in_=self.xown[tc * 128:(tc + 1) * 128, :])
                self.inc(i, "dfx%d" % bi, 16)

            @blk.scalar
            def _(s, tc=tc, bi=bi, g=f_cons[bi]):
                if tc == 0:
                    self.wait(s, "datt0", self.B_end["datt0"])
                    self.wait(s, "datt1", self.B_end["datt1"])
                    for n in ("sv", "ss", "st", "dwd", "dtt"):
                        self.wait(s, n, self.D_end[n])
                self.wait(s, "sv", g)
                i = s.dma_start(out=fa[bi][:], in_=self.attn_s[tc * 128:(tc + 1) * 128, :])
                self.inc(i, "dfa%d" % bi, 16)

            @blk.gpsimd
            def _(gg, tc=tc, bi=bi, g=f_cons[bi]):
                if tc == 0:
                    for n in ("sv", "ss", "st", "dwd", "dy0", "dy1"):
                        self.wait(gg, n, self.D_end[n])
                self.wait(gg, "sv", g)
                i = gg.dma_start(out=fy[bi][:], in_=self.y_s[tc * 128:(tc + 1) * 128, :])
                self.inc(i, "dfy%d" % bi, 16)

            fdx = self.cnt["dfx%d" % bi]
            fda = self.cnt["dfa%d" % bi]
            fdy = self.cnt["dfy%d" % bi]

            @blk.vector
            def _(v, tc=tc, bi=bi, fdx=fdx, fda=fda, fdy=fdy, od=out_dma[bi]):
                self.wait(v, "dfx%d" % bi, fdx)
                self.wait(v, "dfa%d" % bi, fda)
                self.wait(v, "dfy%d" % bi, fdy)
                self.wait(v, "dfo%d" % bi, od)
                i = v.tensor_add(out=fo[bi][:], in0=fx[bi][:], in1=fa[bi][:])
                self.fence(i, v, "sv")
                i = v.tensor_add(out=fo[bi][:], in0=fo[bi][:], in1=fy[bi][:])
                self.inc(i, "sv")
            f_cons[bi] = self.cnt["sv"]
            add_done = self.cnt["sv"]

            if bi == 0:
                @blk.sync
                def _(sy, tc=tc, bi=bi, add_done=add_done):
                    self.wait(sy, "sv", add_done)
                    i = sy.dma_start(out=self.out[tc * 128:(tc + 1) * 128, :], in_=fo[bi][:])
                    self.inc(i, "dfo%d" % bi, 16)
            else:
                @blk.scalar
                def _(s, tc=tc, bi=bi, add_done=add_done):
                    self.wait(s, "sv", add_done)
                    i = s.dma_start(out=self.out[tc * 128:(tc + 1) * 128, :], in_=fo[bi][:])
                    self.inc(i, "dfo%d" % bi, 16)
            out_dma[bi] = self.cnt["dfo%d" % bi]
        f1.close()

        @blk.gpsimd
        def _(gg):
            gg.wait_ge(self.sems["dfo0"], self.cnt["dfo0"])
            gg.wait_ge(self.sems["dfo1"], self.cnt["dfo1"])


def build_program():
    return Prog().build()


_CACHE = {}


def _get_nc():
    if "nc" not in _CACHE:
        _CACHE["nc"] = build_program()
    return _CACHE["nc"]


def _tri_mask():
    kl = np.arange(128)[:, None]
    ql = np.arange(128)[None, :]
    return np.where(kl > ql, np.float32(NEG), np.float32(0.0)).astype(np.float32)


def _masks_for(h):
    m = np.zeros((NSLOT, 2, 128, 128), dtype=np.float32)
    tri = _tri_mask()
    for s in range(NSLOT):
        exact = (s >= 8) if h == 0 else (s < 8)
        if exact:
            m[s, 0] = 0.0
            m[s, 1] = tri
        else:
            m[s, 0] = tri
            m[s, 1] = NEG
    return np.ascontiguousarray(m.transpose(2, 0, 1, 3).reshape(128, NSLOT * 2 * 128))


def _own_tiles(h):
    return [2 * s + h for s in range(8)] + [2 * s + 1 - h for s in range(8, 16)]


def kernel(L0_attn_x, L0_attn_Wq, L0_attn_bq, L0_attn_Wk, L0_attn_bk, L0_attn_Wv,
           L0_attn_bv, L0_ffn_w_gate, L0_ffn_w_up, L0_ffn_w_down,
           L0_ffn_up_reorder_index, L0_ffn_down_reorder_index):
    nc = _get_nc()
    x = np.asarray(L0_attn_x, dtype=np.float32)
    wq = np.ascontiguousarray(np.asarray(L0_attn_Wq, dtype=np.float32))
    bq = np.ascontiguousarray(np.asarray(L0_attn_bq, dtype=np.float32))
    wk = np.ascontiguousarray(np.asarray(L0_attn_Wk, dtype=np.float32))
    bk = np.ascontiguousarray(np.asarray(L0_attn_bk, dtype=np.float32))
    wv = np.ascontiguousarray(np.asarray(L0_attn_Wv, dtype=np.float32))
    bv = np.ascontiguousarray(np.asarray(L0_attn_bv, dtype=np.float32))
    dperm = np.asarray(L0_ffn_down_reorder_index).astype(np.int64)
    wg = np.ascontiguousarray(np.asarray(L0_ffn_w_gate, dtype=np.float32)[dperm])
    wu = np.ascontiguousarray(np.asarray(L0_ffn_w_up, dtype=np.float32)[dperm])
    wd = np.ascontiguousarray(np.asarray(L0_ffn_w_down, dtype=np.float32))
    bvb = np.ascontiguousarray(np.broadcast_to(bv[:, None, :], (G, 128, D))).astype(np.float32)
    iden = np.eye(128, dtype=ml_dtypes.bfloat16)

    in_maps = []
    for c in range(8):
        b, h = c // 2, c % 2
        xs = np.ascontiguousarray(x[b])
        rows = np.concatenate([np.arange(t * 128, (t + 1) * 128) for t in _own_tiles(h)])
        xo = np.ascontiguousarray(xs[rows])
        in_maps.append({
            "xseq": xs, "xown": xo, "wq": wq, "bq": np.ascontiguousarray(bq.T),
            "wk": wk, "bk": np.ascontiguousarray(bk.T),
            "wv": wv, "bvb": bvb, "wg": wg, "wu": wu, "wd": wd,
            "mk": _masks_for(h), "iden": iden,
        })
    res = run_bass_kernel_spmd(nc, in_maps, core_ids=list(range(8)))
    out = np.empty((B, S, E), dtype=np.float32)
    for c in range(8):
        b, h = c // 2, c % 2
        oc = np.asarray(res.results[c]["out"])
        for si, t in enumerate(_own_tiles(h)):
            out[b, t * 128:(t + 1) * 128, :] = oc[si * 128:(si + 1) * 128, :]
    return out
